# version 16
# baseline (speedup 1.0000x reference)
import numpy as np
from contextlib import ExitStack

import concourse.bass as bass
import concourse.mybir as mybir
from concourse.bass_utils import run_bass_kernel_spmd

F32 = mybir.dt.float32
BF16 = mybir.dt.bfloat16
AF = mybir.ActivationFunctionType
ALU = mybir.AluOpType
AX = mybir.AxisListType

DEBUG = globals().get("DEBUG", False)
STAGE = 5
ASUB = 9
PSUB = 9
MOE_ONLY = None
MOE_DIAG = 0
DEBUG2 = False

NCORES = 8
TOK = 2048
NKV = 2304
KC = 8
H = 16
GQ = 512
INTER = 1408
IC_E = 11
IC_S = 22
PAYW = H * 65
LIVE = [(0, 128), (0, 256), (128, 384), (256, 512), (384, 512)]


class Sem:
    def __init__(self, h):
        self.h = h
        self.v = 0

    def inc(self, n=1):
        self.v += n
        return self.v


class StopEmission(Exception):
    pass


class Marks:
    def __init__(self):
        self.m = {}

    def need(self, e, res):
        if res in self.m:
            sem, v = self.m.pop(res)
            e.wait_ge(sem.h, v)

    def set(self, res, sem, v=None):
        self.m[res] = (sem, sem.v if v is None else v)


def build_program():
    nc = bass.Bass("TRN2", target_bir_lowering=False, debug=False,
                   detect_race_conditions=False)

    xin = nc.dram_tensor("xin", [1024, NKV], F32, kind="ExternalInput").ap()
    wqT = nc.dram_tensor("wqT", [1024, 1024], F32, kind="ExternalInput").ap()
    wkT = nc.dram_tensor("wkT", [1024, 1024], F32, kind="ExternalInput").ap()
    wvT = nc.dram_tensor("wvT", [1024, 1024], F32, kind="ExternalInput").ap()
    b3_d = nc.dram_tensor("b3", [128, 24], F32, kind="ExternalInput").ap()
    bvr_d = nc.dram_tensor("bvr", [1, 1024], F32, kind="ExternalInput").ap()
    gw8_d = nc.dram_tensor("gw8", [128, 64], F32, kind="ExternalInput").ap()
    wg_d = nc.dram_tensor("wg", [8, 1024, INTER], F32, kind="ExternalInput").ap()
    wu_d = nc.dram_tensor("wu", [8, 1024, INTER], F32, kind="ExternalInput").ap()
    wd_d = nc.dram_tensor("wd", [8, INTER, 1024], F32, kind="ExternalInput").ap()
    swg_d = nc.dram_tensor("swg", [1024, 2816], F32, kind="ExternalInput").ap()
    swu_d = nc.dram_tensor("swu", [1024, 2816], F32, kind="ExternalInput").ap()
    swd_d = nc.dram_tensor("swd", [2816, 1024], F32, kind="ExternalInput").ap()
    identf_d = nc.dram_tensor("identf", [128, 128], F32, kind="ExternalInput").ap()
    kcorn_d = nc.dram_tensor("kcorn", [2, 128], F32, kind="ExternalInput").ap()
    qcorn_d = nc.dram_tensor("qcorn", [2, 5 * GQ], F32, kind="ExternalInput").ap()
    kfix_d = nc.dram_tensor("kfix", [2, 128], F32, kind="ExternalInput").ap()
    qfix_d = nc.dram_tensor("qfix", [2, TOK], F32, kind="ExternalInput").ap()
    hfix_d = nc.dram_tensor("hfix", [1, NKV], F32, kind="ExternalInput").ap()
    qneg_d = nc.dram_tensor("qnegi", [1, 128], F32, kind="ExternalInput").ap()
    zflag_d = nc.dram_tensor("zflagi", [1, TOK], F32, kind="ExternalInput").ap()
    flags_d = nc.dram_tensor("flags", [128, 2], F32, kind="ExternalInput").ap()
    sel_d = nc.dram_tensor("sel", [8, 8 * 128], F32, kind="ExternalInput").ap()
    yout = nc.dram_tensor("y", [1024, TOK], F32, kind="ExternalOutput").ap()
    cc_in = nc.dram_tensor("cc_in", [128, PAYW], F32)
    cc_out = nc.dram_tensor("cc_out", [128 * NCORES, PAYW], F32,
                            addr_space="Shared")

    dbg = {}
    if DEBUG:
        def dbgt(name, shape):
            dbg[name] = nc.dram_tensor("dbg_" + name, shape, F32, kind="ExternalOutput").ap()
        dbgt("xln1", [1024, NKV])
        dbgt("qkv0", [128, 2 * TOK + 2 * NKV + 18 * 260])
        dbgt("xa", [1024, TOK])
        dbgt("pay", [128, PAYW])
        dbgt("gcomb", [128, PAYW])
        dbgt("wroute", [128, 16 * 8])
        dbgt("lg", [8, TOK])
        dbgt("sct", [128, 16 * 8])
        dbgt("xln2", [1024, TOK])
    if DEBUG2:
        dbg["norm0"] = nc.dram_tensor("dbg_norm0", [128, 3 * 1024 + 512], F32, kind="ExternalOutput").ap()

    es = ExitStack()

    def mk_sb(scope):
        def f(name, shape, dt):
            return scope.enter_context(nc.sbuf_tensor(name, shape, dt))
        return f

    sb = mk_sb(es)
    R = {}
    with es:
        XKV = sb("XKV", [128, KC * NKV], F32)
        XLN = sb("XLN", [128, KC * NKV], BF16)
        ident_sb = sb("ident_sb", [128, 128], F32)
        ones128 = sb("ones128", [128, 128], BF16)
        onesrow = sb("onesrow", [1, 128], BF16)
        onesrow_f = sb("onesrow_f", [1, 64], F32)
        onescol_f = sb("onescol_f", [128, 1], F32)
        b3_sb = sb("b3_sb", [128, 24], F32)
        bsc = sb("bsc", [128, 8], F32)
        gw_sb = sb("gw_sb", [128, 64], F32)
        WQs = sb("WQs", [128, 8 * 256], BF16)
        WKs = sb("WKs", [128, 8 * 256], BF16)
        flags = sb("flags_sb", [128, 2], F32)
        sel_sb = sb("sel_sb", [8, 8 * 128], BF16)
        w_bf = sb("w_bf", [8, TOK], BF16)
        eps_sb = sb("eps_sb", [128, 1], F32)

        P = [es.enter_context(nc.psum_tensor(f"P{i}", [128, 512], F32)) for i in range(8)]

        s_in = Sem(es.enter_context(nc.semaphore("s_in")))
        s_w = Sem(es.enter_context(nc.semaphore("s_w")))
        s_mm = Sem(es.enter_context(nc.semaphore("s_mm")))
        s_sc = Sem(es.enter_context(nc.semaphore("s_sc")))
        s_vc = Sem(es.enter_context(nc.semaphore("s_vc")))
        s_cc = Sem(es.enter_context(nc.semaphore("s_cc")))
        s_st = Sem(es.enter_context(nc.semaphore("s_st")))

        marks = Marks()

        with nc.Block() as blk:
            @blk.sync
            def _(e):
                for k in range(KC):
                    e.dma_start(out=XKV[:, NKV * k:NKV * (k + 1)],
                                in_=xin[128 * k:128 * (k + 1), :]).then_inc(s_in.h, 16)
                    s_in.inc(16)
                for dst, src in ((ident_sb, identf_d), (b3_sb, b3_d), (gw_sb, gw8_d),
                                 (flags, flags_d)):
                    e.dma_start(out=dst[:], in_=src[:, :]).then_inc(s_in.h, 16)
                    s_in.inc(16)
            in_base = s_in.v

            @blk.gpsimd
            def _(e):
                e.dma_start(out=sel_sb[:], in_=sel_d[:, :]).then_inc(s_w.h, 16)
                s_w.inc(16)
            sel_w = s_w.v

            @blk.vector
            def _(e):
                e.memset(ones128[:], 1.0)
                e.memset(onesrow[:], 1.0)
                e.memset(onesrow_f[:], 1.0)
                e.memset(onescol_f[:], 1.0)
                e.memset(eps_sb[:], 1e-5).then_inc(s_vc.h, 1)
                s_vc.inc()
            ones_ready = s_vc.v

            @blk.scalar
            def _(e):
                e.wait_ge(s_in.h, in_base)
                e.mul(out=bsc[:], in_=b3_sb[:, 0:8], mul=0.125).then_inc(s_sc.h, 1)
                s_sc.inc()
            bsc_ready = s_sc.v

            def ln_setup(scope_sb, col0, ncols, tag, slices=None):
                if slices is None:
                    slices = []
                    c = 0
                    while c < ncols:
                        w = min(512, ncols - c)
                        slices.append((col0 + c, w))
                        c += w
                inv = scope_sb(f"inv_{tag}", [128, ncols], F32)
                mean = scope_sb(f"mean_{tag}", [128, ncols], F32)
                sqb = scope_sb(f"sq_{tag}", [128, 2 * KC * 512], BF16)
                t1b = scope_sb(f"t1_{tag}", [128, 2 * 512], F32)
                var_b = scope_sb(f"var_{tag}", [128, 2 * 512], F32)
                sd_b = scope_sb(f"sd_{tag}", [128, 2 * 512], F32)
                tmpb = scope_sb(f"tmp_{tag}", [128, 2 * 512], F32)
                apply_done = {}

                def emit_pair(pr, wait_vc):
                    sq_ready = {}
                    stat_done = {}
                    fv = {}
                    sd_done = {}

                    @blk.scalar
                    def _(e):
                        e.wait_ge(s_in.h, in_base)
                        if wait_vc:
                            e.wait_ge(s_vc.h, wait_vc)
                        for ns in pr:
                            c0, w = slices[ns]
                            for k in range(KC):
                                e.copy(out=XLN[:, NKV * k + c0:NKV * k + c0 + w],
                                       in_=XKV[:, NKV * k + c0:NKV * k + c0 + w])
                        for kc in range(KC):
                            for ns in pr:
                                c0, w = slices[ns]
                                buf = kc * 2 + (ns % 2)
                                e.activation(out=sqb[:, 512 * buf:512 * buf + w],
                                             in_=XLN[:, NKV * kc + c0:NKV * kc + c0 + w],
                                             func=AF.Square).then_inc(s_sc.h, 1)
                                sq_ready[(ns, kc)] = s_sc.inc()

                    @blk.tensor
                    def _(e):
                        e.wait_ge(s_vc.h, ones_ready)
                        for ns in pr:
                            marks.need(e, ("bank", 2 * (ns % 2)))
                            marks.need(e, ("bank", 2 * (ns % 2) + 1))
                        last = None
                        for kc in range(KC):
                            for ns in pr:
                                c0, w = slices[ns]
                                buf = kc * 2 + (ns % 2)
                                e.wait_ge(s_sc.h, sq_ready[(ns, kc)])
                                e.matmul(P[2 * (ns % 2)][:, 0:w], ones128[:],
                                         XLN[:, NKV * kc + c0:NKV * kc + c0 + w],
                                         start=(kc == 0), stop=(kc == KC - 1))
                                last = e.matmul(P[2 * (ns % 2) + 1][:, 0:w], ones128[:],
                                         sqb[:, 512 * buf:512 * buf + w],
                                         start=(kc == 0), stop=(kc == KC - 1))
                        last.then_inc(s_mm.h, 1)
                        s_mm.inc()
                        for ns in pr:
                            stat_done[ns] = s_mm.v

                    @blk.vector
                    def _(e):
                        for ns in pr:
                            e.wait_ge(s_mm.h, stat_done[ns])
                        for ns in pr:
                            c0, w = slices[ns]
                            lc = c0 - col0
                            e.tensor_scalar(out=mean[:, lc:lc + w], in0=P[2 * (ns % 2)][:, 0:w],
                                            scalar1=1.0 / 1024.0, scalar2=None, op0=ALU.mult)
                        if len(pr) == 1:
                            e.drain()
                        for ns in pr:
                            c0, w = slices[ns]
                            lc = c0 - col0
                            e.tensor_tensor(out=t1b[:, 512 * (ns % 2):512 * (ns % 2) + w],
                                            in0=P[2 * (ns % 2)][:, 0:w],
                                            in1=mean[:, lc:lc + w], op=ALU.mult)
                        if len(pr) == 1:
                            e.drain()
                        for ns in pr:
                            c0, w = slices[ns]
                            e.tensor_tensor(out=var_b[:, 512 * (ns % 2):512 * (ns % 2) + w],
                                            in0=P[2 * (ns % 2) + 1][:, 0:w],
                                            in1=t1b[:, 512 * (ns % 2):512 * (ns % 2) + w],
                                            op=ALU.subtract).then_inc(s_vc.h, 1)
                            fv[ns] = s_vc.inc()
                            marks.set(("bank", 2 * (ns % 2)), s_vc)
                            marks.set(("bank", 2 * (ns % 2) + 1), s_vc)

                    @blk.scalar
                    def _(e):
                        for ns in pr:
                            e.wait_ge(s_vc.h, fv[ns])
                            e.activation(out=sd_b[:, 512 * (ns % 2):512 * (ns % 2) + slices[ns][1]],
                                         in_=var_b[:, 512 * (ns % 2):512 * (ns % 2) + slices[ns][1]],
                                         func=AF.Sqrt, bias=eps_sb[:, 0:1], scale=1.0 / 1024.0).then_inc(s_sc.h, 1)
                            sd_done[ns] = s_sc.inc()

                    @blk.vector
                    def _(e):
                        for ns in pr:
                            e.wait_ge(s_sc.h, sd_done[ns])
                        for ns in pr:
                            c0, w = slices[ns]
                            lc = c0 - col0
                            e.reciprocal(out=inv[:, lc:lc + w],
                                         in_=sd_b[:, 512 * (ns % 2):512 * (ns % 2) + w])
                        for ns in pr:
                            c0, w = slices[ns]
                            lc = c0 - col0
                            prev = None
                            for kc in range(KC):
                                e.tensor_tensor(out=tmpb[:, 512 * (kc % 2):512 * (kc % 2) + w],
                                                in0=XKV[:, NKV * kc + c0:NKV * kc + c0 + w],
                                                in1=mean[:, lc:lc + w], op=ALU.subtract)
                                if prev is not None:
                                    e.tensor_tensor(out=XLN[:, NKV * prev + c0:NKV * prev + c0 + w],
                                                    in0=tmpb[:, 512 * (prev % 2):512 * (prev % 2) + w],
                                                    in1=inv[:, lc:lc + w], op=ALU.mult)
                                prev = kc
                            e.tensor_tensor(out=XLN[:, NKV * prev + c0:NKV * prev + c0 + w],
                                            in0=tmpb[:, 512 * (prev % 2):512 * (prev % 2) + w],
                                            in1=inv[:, lc:lc + w], op=ALU.mult).then_inc(s_vc.h, 1)
                            apply_done[ns] = s_vc.inc()
                return inv, mean, apply_done, emit_pair, len(slices)

            ln1_scope = ExitStack()
            inv1, mean1, ln1_done, ln1_pair, ln1_ns = ln_setup(mk_sb(ln1_scope), 0, NKV, "ln1")
            for pair0 in range(0, ln1_ns, 2):
                ln1_pair([q for q in (pair0, pair0 + 1) if q < ln1_ns], None)
            ln1_all = max(ln1_done.values())

            if DEBUG:
                @blk.gpsimd
                def _(e):
                    e.wait_ge(s_vc.h, ln1_all)
                    for k in range(KC):
                        e.dma_start(out=dbg["xln1"][128 * k:128 * (k + 1), :],
                                    in_=XLN[:, NKV * k:NKV * (k + 1)]).then_inc(s_st.h, 16)
                        s_st.inc(16)

            ln1_scope.close()

            final_wait = [ln1_all]

            def emit_rest():
                if STAGE < 2:
                    raise StopEmission
                attn_scope = ExitStack()
                open_scopes.append(attn_scope)
                sbA = mk_sb(attn_scope)
                WVs = sbA("WVs", [128, 8 * 256], BF16)
                QSB = sbA("QSB", [128, 2 * TOK], BF16)
                KSB = sbA("KSB", [128, 2 * NKV], BF16)
                VSB = sbA("VSB", [128, 18 * 260], BF16)
                QG = sbA("QG", [128, 2 * 128], BF16)
                EB = sbA("EB", [128, 12 * 512], BF16)
                EG = sbA("EG", [128, 17 * 128], BF16)
                CTU = sbA("CTU", [128, 4 * 512], BF16)
                RR = sbA("RR", [1, 4 * 512], BF16)
                TPH = sbA("TPH", [128, 128], F32)
                payload = sbA("payload", [128, PAYW], F32)
                kcorn = sbA("kcorn_sb", [2, 128], BF16)
                qcorn = sbA("qcorn_sb", [2, 5 * GQ], BF16)
                kfix = sbA("kfix_sb", [2, 128], BF16)
                qfix = sbA("qfix_sb", [2, TOK], BF16)
                hfix = sbA("hfix_sb", [1, NKV], BF16)
                qneg = sbA("qneg_sb", [1, 128], BF16)
                zflag = sbA("zflag_sb", [1, TOK], BF16)
                bvr_sb = sbA("bvr_sb", [1, 1024], BF16)
                wq3 = WQs[:].rearrange("p (k c) -> p k c", c=256)
                wk3 = WKs[:].rearrange("p (k c) -> p k c", c=256)
                wv3 = WVs[:].rearrange("p (k c) -> p k c", c=256)
                vsb4 = VSB[:].rearrange("p (t h c) -> p t h c", h=4, c=65)
                qc3 = qcorn[:].rearrange("p (c q) -> p c q", q=GQ)

                aconst_w = [None]

                @blk.vector
                def _(e):
                    e.memset(vsb4[:, :, :, 64:65], 1.0).then_inc(s_vc.h, 1)
                    s_vc.inc()
                vones = s_vc.v

                pay_last = [None]
                track_mid = [0]
                v_evac_last = [0]
                pend_norm = [None]

                def emit_norm_finish(rb, hb, sl0, gp, oc, qd):
                    rbm = {}

                    @blk.tensor
                    def _(e):
                        for gi in range(2):
                            bank = 6 + gi
                            marks.need(e, ("bank", bank))
                            e.wait_ge(s_vc.h, rb[gi])
                            e.matmul(P[bank][hb:hb + 64, 0:512], onesrow[0:1, 0:64],
                                     RR[0:1, 512 * (sl0 + gi):512 * (sl0 + gi) + 512],
                                     start=True, stop=True).then_inc(s_mm.h, 1)
                            rbm[gi] = s_mm.inc()

                    @blk.vector
                    def _(e):
                        with nc.allow_low_precision(reason="bf16 ctx staging; residual add stays fp32"):
                            for gi in range(2):
                                e.wait_ge(s_mm.h, rbm[gi])
                                e.tensor_tensor(out=CTU[hb:hb + 64, 512 * (sl0 + gi):512 * (sl0 + gi + 1)],
                                                in0=CTU[hb:hb + 64, 512 * (sl0 + gi):512 * (sl0 + gi + 1)],
                                                in1=P[6 + gi][hb:hb + 64, 0:512], op=ALU.mult)
                        for gi in range(2):
                            g = 2 * gp + gi
                            xsl = XKV[hb:hb + 64,
                                      NKV * (2 * qd + oc) + 64 + GQ * g:NKV * (2 * qd + oc) + 64 + GQ * (g + 1)]
                            e.tensor_tensor(out=xsl, in0=CTU[hb:hb + 64, 512 * (sl0 + gi):512 * (sl0 + gi + 1)],
                                            in1=xsl, op=ALU.add).then_inc(s_vc.h, 1)
                            s_vc.inc()
                            marks.set(("bank", 6 + gi), s_vc)

                for qd in range(4):
                    if PSUB < 1:
                        break
                    @blk.gpsimd
                    def _(e):
                        c0 = 256 * qd
                        marks.need(e, "wslice")
                        for buf3, src in ((wq3, wqT), (wk3, wkT)):
                            e.dma_start(out=buf3,
                                        in_=src[:, c0:c0 + 256].rearrange("(k p) c -> p k c", p=128)
                                        ).then_inc(s_w.h, 16)
                            s_w.inc(16)
                        if qd == 0:
                            e.wait_ge(s_vc.h, ln1_all)
                        e.dma_start(out=wv3,
                                    in_=wvT[:, c0:c0 + 256].rearrange("(k p) c -> p k c", p=128)
                                    ).then_inc(s_w.h, 16)
                        s_w.inc(16)
                        if qd == 0:
                            for dst, src in ((kcorn, kcorn_d), (qcorn, qcorn_d), (kfix, kfix_d),
                                             (qfix, qfix_d), (hfix, hfix_d), (zflag, zflag_d),
                                             (bvr_sb, bvr_d), (qneg, qneg_d)):
                                e.dma_start(out=dst[:], in_=src[:, :]).then_inc(s_w.h, 16)
                                s_w.inc(16)
                            aconst_w[0] = s_w.v
                    w_ready = s_w.v

                    proj = {}

                    sc0 = s_sc.v
                    vv0 = s_vc.v

                    @blk.tensor
                    def _(e):
                        if PSUB == 11:
                            return
                        e.wait_ge(s_w.h, w_ready)
                        if qd == 0:
                            e.wait_ge(s_sc.h, bsc_ready)
                            e.wait_ge(s_w.h, aconst_w[0])
                            e.wait_ge(s_vc.h, vones)
                        marks.need(e, "qsb")
                        marks.need(e, "ksb")
                        marks.need(e, "vsb")
                        for i in range(8):
                            oc, ts_ = divmod(i, 4)
                            bank = 2 + i % 2
                            if qd == 0:
                                e.wait_ge(s_vc.h, ln1_done[min(ts_ + 1, 4)])
                            if i < 2:
                                marks.need(e, ("bank", bank))
                            else:
                                e.wait_ge(s_sc.h, sc0 + i - 1)
                            last = None
                            for kc in range(KC):
                                last = e.matmul(P[bank][:, 0:512],
                                         wq3[:, kc, 128 * oc:128 * (oc + 1)],
                                         XLN[:, NKV * kc + 64 + 512 * ts_:NKV * kc + 64 + 512 * (ts_ + 1)],
                                         start=(kc == 0), stop=(kc == KC - 1))
                            last.then_inc(s_mm.h, 1)
                            proj[("q", i)] = (s_mm.inc(), bank)
                        if qd == 0:
                            e.wait_ge(s_vc.h, ln1_all)
                        for i in range(10):
                            oc, ts_ = divmod(i, 5)
                            c0 = 512 * ts_
                            w = min(512, NKV - c0)
                            bank = 2 + i % 2
                            e.wait_ge(s_sc.h, sc0 + 8 + i - 1)
                            last = None
                            for kc in range(KC):
                                last = e.matmul(P[bank][:, 0:w],
                                         wk3[:, kc, 128 * oc:128 * (oc + 1)],
                                         XLN[:, NKV * kc + c0:NKV * kc + c0 + w],
                                         start=(kc == 0), stop=(kc == KC - 1))
                            last.then_inc(s_mm.h, 1)
                            proj[("k", i)] = (s_mm.inc(), bank)
                        for tc in range(18):
                            bank = tc % 2
                            if tc < 2:
                                marks.need(e, ("bank", bank))
                            else:
                                e.wait_ge(s_vc.h, vv0 + tc - 1)
                            for kc in range(KC):
                                e.matmul(P[bank][:, 0:256],
                                         XLN[:, NKV * kc + 128 * tc:NKV * kc + 128 * (tc + 1)],
                                         wv3[:, kc, :],
                                         start=(kc == 0), stop=False)
                            e.matmul(P[bank][:, 0:256], onesrow[:],
                                     bvr_sb[0:1, 256 * qd:256 * (qd + 1)],
                                     start=False, stop=True).then_inc(s_mm.h, 1)
                            proj[("v", tc)] = (s_mm.inc(), bank)
                        for oc in range(2):
                            bank = 2 + oc % 2
                            e.wait_ge(s_sc.h, sc0 + 18 + oc - 1)
                            last = None
                            for kc in range(KC):
                                last = e.matmul(P[bank][:, 0:128],
                                         wq3[:, kc, 128 * oc:128 * (oc + 1)],
                                         XLN[:, NKV * kc + 2176:NKV * kc + 2304],
                                         start=(kc == 0), stop=(kc == KC - 1))
                            last.then_inc(s_mm.h, 1)
                            proj[("qg", oc)] = (s_mm.inc(), bank)
                        marks.set("wslice", s_mm)

                    @blk.scalar
                    def _(e):
                        for i in range(8):
                            oc, ts_ = divmod(i, 4)
                            v, bank = proj[("q", i)]
                            e.wait_ge(s_mm.h, v)
                            e.activation(out=QSB[:, TOK * oc + 512 * ts_:TOK * oc + 512 * (ts_ + 1)],
                                         in_=P[bank][:, 0:512], func=AF.Identity,
                                         bias=bsc[:, 2 * qd + oc:2 * qd + oc + 1],
                                         scale=0.125).then_inc(s_sc.h, 1)
                            s_sc.inc()
                            marks.set(("bank", bank), s_sc)
                        for i in range(10):
                            oc, ts_ = divmod(i, 5)
                            c0 = 512 * ts_
                            w = min(512, NKV - c0)
                            v, bank = proj[("k", i)]
                            e.wait_ge(s_mm.h, v)
                            e.activation(out=KSB[:, NKV * oc + c0:NKV * oc + c0 + w],
                                         in_=P[bank][:, 0:w], func=AF.Identity,
                                         bias=b3_sb[:, 8 + 2 * qd + oc:8 + 2 * qd + oc + 1],
                                         scale=1.0).then_inc(s_sc.h, 1)
                            s_sc.inc()
                            marks.set(("bank", bank), s_sc)
                        for oc in range(2):
                            v, bank = proj[("qg", oc)]
                            e.wait_ge(s_mm.h, v)
                            e.activation(out=QG[:, 128 * oc:128 * (oc + 1)],
                                         in_=P[bank][:, 0:128], func=AF.Identity,
                                         bias=bsc[:, 2 * qd + oc:2 * qd + oc + 1],
                                         scale=0.125).then_inc(s_sc.h, 1)
                            s_sc.inc()
                            marks.set(("bank", bank), s_sc)
                    qk_evac = s_sc.v

                    @blk.vector
                    def _(e):
                        for tc in range(18):
                            v, bank = proj[("v", tc)]
                            e.wait_ge(s_mm.h, v)
                            e.tensor_copy(out=vsb4[:, tc, :, 0:64],
                                          in_=P[bank][:, 0:256].rearrange("p (h c) -> p h c", c=64)
                                          ).then_inc(s_vc.h, 1)
                            s_vc.inc()
                            marks.set(("bank", bank), s_vc)
                    v_evac = s_vc.v
                    v_evac_last[0] = v_evac

                    if DEBUG and qd == 0:
                        @blk.gpsimd
                        def _(e):
                            e.wait_ge(s_sc.h, qk_evac)
                            e.wait_ge(s_vc.h, v_evac)
                            e.dma_start(out=dbg["qkv0"][:, 0:2 * TOK], in_=QSB[:]).then_inc(s_st.h, 16)
                            e.dma_start(out=dbg["qkv0"][:, 2 * TOK:2 * TOK + 2 * NKV], in_=KSB[:]).then_inc(s_st.h, 16)
                            e.dma_start(out=dbg["qkv0"][:, 2 * TOK + 2 * NKV:], in_=VSB[:]).then_inc(s_st.h, 16)
                            s_st.inc(48)

                    for hl in range(4):
                        if ASUB < 2:
                            continue
                        h = 4 * qd + hl
                        oc, half = divmod(hl, 2)
                        Qrow = QSB[64 * half:64 * half + 64, TOK * oc:TOK * (oc + 1)]
                        Krow = KSB[64 * half:64 * half + 64, NKV * oc:NKV * (oc + 1)]
                        QGrow = QG[64 * half:64 * half + 64, 128 * oc:128 * (oc + 1)]

                        for gp in range(2):
                            ginfo = {}
                            smarks = {0: {}, 1: {}}
                            emarks = {0: {}, 1: {}}
                            sc00 = s_sc.v
                            for gi in range(2):
                                g = 2 * gp + gi
                                par = (h * 4 + g) % 2
                                smark = smarks[gi]

                                @blk.tensor
                                def _(e, g=g, gi=gi, smark=smark):
                                    e.wait_ge(s_sc.h, qk_evac)
                                    e.wait_ge(s_vc.h, v_evac)
                                    for c in range(6):
                                        l = gi * 6 + c
                                        sbank = (0, 1, 4, 5)[l % 4]
                                        if l < 4:
                                            marks.need(e, ("bank", sbank))
                                        else:
                                            e.wait_ge(s_sc.h, sc00 + l - 3)
                                        if c == 0:
                                            lo, hi = 0, 512
                                            e.matmul(P[sbank][:, lo:hi], Krow[:, 2176:2304],
                                                     Qrow[:, GQ * g + lo:GQ * g + hi], start=True, stop=False)
                                            e.matmul(P[sbank][:, lo:hi], kfix[:],
                                                     qfix[:, GQ * g + lo:GQ * g + hi],
                                                     start=False, stop=True).then_inc(s_mm.h, 1)
                                        else:
                                            cw = c - 1
                                            k0 = GQ * g + 128 * cw
                                            lo, hi = LIVE[cw]
                                            e.matmul(P[sbank][:, lo:hi], Krow[:, k0:k0 + 128],
                                                     Qrow[:, GQ * g + lo:GQ * g + hi], start=True, stop=False)
                                            e.matmul(P[sbank][:, lo:hi], kcorn[:],
                                                     qc3[:, cw, lo:hi],
                                                     start=False, stop=True).then_inc(s_mm.h, 1)
                                        smark[c] = (s_mm.inc(), sbank)

                            for gi in range(2):
                                g = 2 * gp + gi
                                par = (h * 4 + g) % 2
                                smark = smarks[gi]
                                emark = emarks[gi]

                                @blk.scalar
                                def _(e, par=par, smark=smark, emark=emark):
                                    marks.need(e, ("eb", par))
                                    for c in range(6):
                                        v, sbank = smark[c]
                                        lo, hi = (0, 512) if c == 0 else LIVE[c - 1]
                                        e.wait_ge(s_mm.h, v)
                                        e.activation(out=EB[:, 512 * (6 * par + c) + lo:512 * (6 * par + c) + hi],
                                                     in_=P[sbank][:, lo:hi], func=AF.Exp).then_inc(s_sc.h, 1)
                                        emark[c] = s_sc.inc()
                                        marks.set(("bank", sbank), s_sc)

                            for gi in range(2):
                                g = 2 * gp + gi
                                pcb = 2 + gi
                                par = (h * 4 + g) % 2
                                emark = emarks[gi]

                                @blk.tensor
                                def _(e, g=g, gi=gi, pcb=pcb, par=par, emark=emark, hl=hl):
                                    marks.need(e, ("bank", pcb))
                                    for c in range(6):
                                        lo, hi = (0, 512) if c == 0 else LIVE[c - 1]
                                        tc = 17 if c == 0 else 4 * g + (c - 1)
                                        e.wait_ge(s_sc.h, emark[c])
                                        e.matmul(P[pcb][0:65, lo:hi],
                                                 vsb4[:, tc, hl, :],
                                                 EB[:, 512 * (6 * par + c) + lo:512 * (6 * par + c) + hi],
                                                 start=(c == 0), stop=(c == 5)).then_inc(s_mm.h, 1)
                                        s_mm.inc()
                                    ginfo[gi] = (s_mm.v, pcb)
                                    marks.set(("eb", par), s_mm)

                            rb = {}
                            hb = 64 * half
                            sl0 = 2 * ((h * 2 + gp) % 2)

                            @blk.vector
                            def _(e, rb=rb, hb=hb, sl0=sl0, gp=gp):
                                for gi in range(2):
                                    v, pcb = ginfo[gi]
                                    e.wait_ge(s_mm.h, v)
                                with nc.allow_low_precision(reason="bf16 softmax recip staging"):
                                    for gi in range(2):
                                        _, pcb = ginfo[gi]
                                        e.reciprocal(out=RR[:, 512 * (sl0 + gi):512 * (sl0 + gi) + 512],
                                                     in_=P[pcb][64:65, 0:512])
                                for gi in range(2):
                                    _, pcb = ginfo[gi]
                                    e.tensor_copy(out=CTU[hb:hb + 64, 512 * (sl0 + gi):512 * (sl0 + gi + 1)],
                                                  in_=P[pcb][0:64, 0:512])
                                with nc.allow_low_precision(reason="bf16 softmax recip staging; fp32 accum in PSUM"):
                                    for gi in range(2):
                                        g = 2 * gp + gi
                                        e.tensor_tensor(out=RR[:, 512 * (sl0 + gi):512 * (sl0 + gi) + 512],
                                                        in0=RR[:, 512 * (sl0 + gi):512 * (sl0 + gi) + 512],
                                                        in1=zflag[:, GQ * g:GQ * (g + 1)],
                                                        op=ALU.mult).then_inc(s_vc.h, 1)
                                        rb[gi] = s_vc.inc()
                                for gi in range(2):
                                    _, pcb = ginfo[gi]
                                    marks.set(("bank", pcb), s_vc, rb[gi])

                            if pend_norm[0] is not None:
                                emit_norm_finish(*pend_norm[0])
                            pend_norm[0] = (rb, hb, sl0, gp, oc, qd)


                    if ASUB == 2:
                        track_mid[0] = s_vc.v
                    for hl in range(4):
                        if ASUB < 3:
                            break
                        h = 4 * qd + hl
                        oc, half = divmod(hl, 2)
                        Krow = KSB[64 * half:64 * half + 64, NKV * oc:NKV * (oc + 1)]
                        QGrow = QG[64 * half:64 * half + 64, 128 * oc:128 * (oc + 1)]
                        gs = {}

                        if hl == 1 and pend_norm[0] is not None:
                            emit_norm_finish(*pend_norm[0])
                            pend_norm[0] = None
                        scg0 = s_sc.v

                        @blk.tensor
                        def _(e, scg0=scg0):
                            for r in range(5):
                                bank = 4 + r % 2
                                if r < 2:
                                    marks.need(e, ("bank", bank))
                                else:
                                    e.wait_ge(s_sc.h, scg0 + r - 1)
                                nch = 4 if r < 4 else 1
                                last = None
                                for cc in range(nch):
                                    kchunk = 4 * r + cc
                                    k0 = 128 * kchunk
                                    last = e.matmul(P[bank][:, 128 * cc:128 * (cc + 1)],
                                             Krow[:, k0:k0 + 128], QGrow,
                                             start=True, stop=(kchunk not in (0, 16)))
                                    if kchunk in (0, 16):
                                        last = e.matmul(P[bank][:, 128 * cc:128 * (cc + 1)],
                                                 hfix[0:1, k0:k0 + 128], qneg[:],
                                                 start=False, stop=True)
                                last.then_inc(s_mm.h, 1)
                                gs[r] = (s_mm.inc(), bank)

                        ge_ = {}

                        @blk.scalar
                        def _(e):
                            marks.need(e, ("eg", 0))
                            for r in range(5):
                                v, bank = gs[r]
                                w = 512 if r < 4 else 128
                                e.wait_ge(s_mm.h, v)
                                e.activation(out=EG[:, 512 * r:512 * r + w],
                                             in_=P[bank][:, 0:w], func=AF.Exp).then_inc(s_sc.h, 1)
                                ge_[r] = s_sc.inc()
                                marks.set(("bank", bank), s_sc)

                        @blk.tensor
                        def _(e, hl=hl):
                            marks.need(e, ("bank", 2))
                            e.wait_ge(s_sc.h, ge_[4])
                            last = None
                            for kchunk in range(17):
                                last = e.matmul(P[2][0:65, 0:128], vsb4[:, kchunk, hl, :],
                                         EG[:, 128 * kchunk:128 * (kchunk + 1)],
                                         start=(kchunk == 0), stop=(kchunk == 16))
                            last.then_inc(s_mm.h, 1)
                            R["pcg"] = s_mm.inc()
                            marks.set(("eg", 0), s_mm)

                        @blk.vector
                        def _(e):
                            e.wait_ge(s_mm.h, R["pcg"])
                            marks.need(e, ("tph", 0))
                            e.tensor_copy(out=TPH[0:65, :], in_=P[2][0:65, 0:128]).then_inc(s_vc.h, 1)
                            R["tph"] = s_vc.inc()
                            marks.set(("bank", 2), s_vc)

                        @blk.tensor
                        def _(e):
                            marks.need(e, ("bank", 3))
                            e.wait_ge(s_vc.h, R["tph"])
                            e.transpose(P[3][:, 0:65], TPH[0:65, :], ident_sb[0:65, 0:65]).then_inc(s_mm.h, 1)
                            R["trm"] = s_mm.inc()
                            marks.set(("tph", 0), s_mm)

                        @blk.vector
                        def _(e, h=h):
                            e.wait_ge(s_mm.h, R["trm"])
                            e.tensor_copy(out=payload[:, 65 * h:65 * (h + 1)],
                                          in_=P[3][:, 0:65]).then_inc(s_vc.h, 1)
                            pay_last[0] = s_vc.inc()
                            marks.set(("bank", 3), s_vc)

                    marks.set("projw", s_mm)
                    marks.set("qsb", s_mm)
                    marks.set("ksb", s_mm)
                    marks.set("vsb", s_mm)

                if DEBUG and ASUB >= 3:
                    @blk.sync
                    def _(e):
                        e.wait_ge(s_vc.h, pay_last[0])
                        e.dma_start(out=dbg["pay"][:, :], in_=payload[:]).then_inc(s_st.h, 16)
                        s_st.inc(16)

                if pend_norm[0] is not None:
                    emit_norm_finish(*pend_norm[0])
                    pend_norm[0] = None
                mid_done = s_vc.v

                if ASUB < 4:
                    final_wait[0] = (pay_last[0] if ASUB == 3 else
                                     track_mid[0] if ASUB == 2 else v_evac_last[0])
                    raise StopEmission
                @blk.sync
                def _(e):
                    e.wait_ge(s_vc.h, pay_last[0])
                    e.dma_start(out=cc_in.ap()[:, :], in_=payload[:]).then_inc(s_st.h, 16)
                    s_st.inc(16)
                pay_in = s_st.v

                @blk.gpsimd
                def _(e):
                    e.wait_ge(s_st.h, pay_in)
                    e.collective_compute("AllGather", ALU.bypass,
                                         replica_groups=[list(range(NCORES))],
                                         ins=[cc_in.ap().opt()],
                                         outs=[cc_out.ap().opt()]).then_inc(s_cc.h, 1)
                    s_cc.inc()
                cc_done = s_cc.v

                attn_scope.close()
                open_scopes.remove(attn_scope)

                ln2_scope = ExitStack()
                open_scopes.append(ln2_scope)
                inv2, mean2, ln2_done, ln2_pair, _ = ln_setup(
                    mk_sb(ln2_scope), 64, TOK, "ln2",
                    slices=[(64, 64), (128, 448), (576, 512),
                            (1088, 512), (1600, 448), (2048, 64)])
                rt_scope = ExitStack()
                open_scopes.append(rt_scope)
                sbR = mk_sb(rt_scope)
                G1N = sbR("G1N", [8, 1], F32)
                LG = sbR("LG", [8, TOK], F32)
                SCT = sbR("SCT", [128, 16 * 8], F32)
                MX = sbR("MX", [128, 16], F32)
                EXT = sbR("EXT", [128, 16 * 8], F32)
                M1T = sbR("M1T", [128, 16], F32)
                M2T = sbR("M2T", [128, 16], F32)
                MSK = sbR("MSK", [128, 16 * 8], F32)
                DEN = sbR("DEN", [128, 16], F32)
                WTM = sbR("WTM", [128, 16 * 8], F32)
                lt = {}
                exd = {}

                def emit_route_logits(tslist, rbanks, wait_xv):
                    lg_mm = {}

                    @blk.tensor
                    def _(e):
                        e.wait_ge(s_vc.h, wait_xv)
                        for i, ts_ in enumerate(tslist):
                            bank = rbanks[i]
                            marks.need(e, ("bank", bank))
                            last = None
                            for kc in range(KC):
                                last = e.matmul(P[bank][0:8, 0:512], gw_sb[:, 8 * kc:8 * (kc + 1)],
                                         XKV[:, NKV * kc + 64 + 512 * ts_:NKV * kc + 64 + 512 * (ts_ + 1)],
                                         start=(kc == 0), stop=(kc == KC - 1))
                            last.then_inc(s_mm.h, 1)
                            lg_mm[ts_] = (s_mm.inc(), bank)
                    return lg_mm

                def emit_route_fold(tslist, lg_mm, ln_waits):
                    @blk.vector
                    def _(e):
                        e.wait_ge(s_vc.h, R["g1done"])
                        for v in ln_waits:
                            e.wait_ge(s_vc.h, v)
                        for ts_ in tslist:
                            e.wait_ge(s_mm.h, lg_mm[ts_][0])
                        for ts_ in tslist:
                            _, bank = lg_mm[ts_]
                            e.scalar_tensor_tensor(out=LG[:, 512 * ts_:512 * (ts_ + 1)],
                                                   in0=mean2[0:8, 512 * ts_:512 * (ts_ + 1)],
                                                   scalar=G1N[:, 0:1],
                                                   in1=P[bank][0:8, 0:512],
                                                   op0=ALU.mult, op1=ALU.add)
                        last = None
                        for ts_ in tslist:
                            last = e.tensor_tensor(out=LG[:, 512 * ts_:512 * (ts_ + 1)],
                                            in0=LG[:, 512 * ts_:512 * (ts_ + 1)],
                                            in1=inv2[0:8, 512 * ts_:512 * (ts_ + 1)], op=ALU.mult)
                        last.then_inc(s_vc.h, 1)
                        s_vc.inc()
                        for ts_ in tslist:
                            marks.set(("bank", lg_mm[ts_][1]), s_vc)

                def emit_route_t(tlist, rbanks2):
                    v0 = s_vc.v

                    @blk.tensor
                    def _(e):
                        for idx, t in enumerate(tlist):
                            bank = rbanks2[idx % 2]
                            if idx < 2:
                                marks.need(e, ("bank", bank))
                            else:
                                e.wait_ge(s_vc.h, v0 + idx - 1)
                            e.transpose(P[bank][:, 0:8], LG[0:8, 128 * t:128 * (t + 1)],
                                        ident_sb[0:8, 0:8]).then_inc(s_mm.h, 1)
                            lt[t] = (s_mm.inc(), bank)

                    @blk.vector
                    def _(e):
                        for t in tlist:
                            v, bank = lt[t]
                            e.wait_ge(s_mm.h, v)
                            e.tensor_copy(out=SCT[:, 8 * t:8 * (t + 1)],
                                          in_=P[bank][:, 0:8]).then_inc(s_vc.h, 1)
                            s_vc.inc()
                            marks.set(("bank", bank), s_vc)
                        for t in tlist:
                            e.reduce_max(out=MX[:, t:t + 1], in_=SCT[:, 8 * t:8 * (t + 1)],
                                         axis=AX.X, negate=True).then_inc(s_vc.h, 1)
                            s_vc.inc()
                    grp_done = s_vc.v

                    @blk.scalar
                    def _(e):
                        e.wait_ge(s_vc.h, grp_done)
                        for t in tlist:
                            e.activation(out=EXT[:, 8 * t:8 * (t + 1)], in_=SCT[:, 8 * t:8 * (t + 1)],
                                         func=AF.Exp, bias=MX[:, t:t + 1], scale=1.0).then_inc(s_sc.h, 1)
                            exd[t] = s_sc.inc()

                    @blk.vector
                    def _(e):
                        e.wait_ge(s_sc.h, max(exd[t] for t in tlist))
                        for t in tlist:
                            e.reduce_max(out=M1T[:, t:t + 1], in_=EXT[:, 8 * t:8 * (t + 1)], axis=AX.X)
                        for t in tlist:
                            e.tensor_scalar(out=MSK[:, 8 * t:8 * (t + 1)], in0=EXT[:, 8 * t:8 * (t + 1)],
                                            scalar1=M1T[:, t:t + 1], scalar2=None, op0=ALU.is_ge)
                        for t in tlist:
                            e.scalar_tensor_tensor(out=WTM[:, 8 * t:8 * (t + 1)],
                                                   in0=MSK[:, 8 * t:8 * (t + 1)], scalar=-1e30,
                                                   in1=EXT[:, 8 * t:8 * (t + 1)], op0=ALU.mult, op1=ALU.add)
                        for t in tlist:
                            e.reduce_max(out=M2T[:, t:t + 1], in_=WTM[:, 8 * t:8 * (t + 1)], axis=AX.X)
                        for t in tlist:
                            e.tensor_tensor(out=DEN[:, t:t + 1], in0=M1T[:, t:t + 1],
                                            in1=M2T[:, t:t + 1], op=ALU.add)
                        for t in tlist:
                            e.tensor_scalar(out=MSK[:, 8 * t:8 * (t + 1)], in0=EXT[:, 8 * t:8 * (t + 1)],
                                            scalar1=M2T[:, t:t + 1], scalar2=None, op0=ALU.is_ge)
                        for t in tlist:
                            e.reciprocal(out=DEN[:, t:t + 1], in_=DEN[:, t:t + 1])
                        for t in tlist:
                            e.scalar_tensor_tensor(out=WTM[:, 8 * t:8 * (t + 1)],
                                                   in0=EXT[:, 8 * t:8 * (t + 1)], scalar=DEN[:, t:t + 1],
                                                   in1=MSK[:, 8 * t:8 * (t + 1)],
                                                   op0=ALU.mult, op1=ALU.mult).then_inc(s_vc.h, 1)
                            s_vc.inc()
                    wtm_grp = s_vc.v

                    wt = {}
                    v0w = s_vc.v

                    @blk.tensor
                    def _(e):
                        e.wait_ge(s_vc.h, wtm_grp)
                        for idx, t in enumerate(tlist):
                            bank = rbanks2[idx % 2]
                            if idx < 2:
                                marks.need(e, ("bank", bank))
                            else:
                                e.wait_ge(s_vc.h, v0w + idx - 1)
                            e.transpose(P[bank][0:8, 0:128], WTM[:, 8 * t:8 * (t + 1)],
                                        ident_sb[:]).then_inc(s_mm.h, 1)
                            wt[t] = (s_mm.inc(), bank)

                    @blk.vector
                    def _(e):
                        for t in tlist:
                            v, bank = wt[t]
                            e.wait_ge(s_mm.h, v)
                            e.tensor_copy(out=w_bf[:, 128 * t:128 * (t + 1)],
                                          in_=P[bank][0:8, 0:128]).then_inc(s_vc.h, 1)
                            s_vc.inc()
                            marks.set(("bank", bank), s_vc)
                    return s_vc.v

                if STAGE >= 4:
                    @blk.tensor
                    def _(e):
                        marks.need(e, ("bank", 6))
                        e.wait_ge(s_vc.h, ones_ready)
                        last = None
                        for kc in range(KC):
                            last = e.matmul(P[6][0:8, 0:1], gw_sb[:, 8 * kc:8 * (kc + 1)], onescol_f[:],
                                     start=(kc == 0), stop=(kc == KC - 1))
                        last.then_inc(s_mm.h, 1)
                        R["g1mm"] = s_mm.inc()

                    @blk.vector
                    def _(e):
                        e.wait_ge(s_mm.h, R["g1mm"])
                        e.tensor_scalar(out=G1N[:], in0=P[6][0:8, 0:1], scalar1=-1.0, scalar2=None,
                                        op0=ALU.mult).then_inc(s_vc.h, 1)
                        R["g1done"] = s_vc.inc()
                        marks.set(("bank", 6), s_vc)

                if STAGE >= 3:
                    ln2_pair([2, 3], mid_done)
                    ln2_pair([1, 4], mid_done)

                if STAGE >= 4:
                    lg_mid = emit_route_logits([1, 2], [6, 7], mid_done)
                    emit_route_fold([1, 2], lg_mid, [ln2_done[2], ln2_done[3]])
                    emit_route_t([4, 5, 6, 7, 8, 9, 10, 11], [6, 7])

                gc_scope = ExitStack()
                open_scopes.append(gc_scope)
                sbG = mk_sb(gc_scope)
                GSL = sbG("GSL", [128, 8 * PAYW], BF16)
                GACC = sbG("GACC", [128, PAYW], F32)
                GRC = sbG("GRC", [128, 16], F32)
                GN = sbG("GN", [128, H * 64], F32)

                @blk.gpsimd
                def _(e):
                    e.wait_ge(s_cc.h, cc_done)
                    for i in range(NCORES):
                        e.dma_start(out=GSL[:, PAYW * i:PAYW * (i + 1)],
                                    in_=cc_out.ap()[128 * i:128 * (i + 1), :]).then_inc(s_w.h, 16)
                        s_w.inc(16)
                gsl_in = s_w.v

                @blk.vector
                def _(e):
                    HW = 8 * 65
                    e.wait_ge(s_w.h, gsl_in)
                    e.tensor_tensor(out=GACC[:, 0:HW], in0=GSL[:, 0:HW],
                                    in1=GSL[:, PAYW:PAYW + HW], op=ALU.add)
                    e.tensor_tensor(out=GACC[:, HW:PAYW], in0=GSL[:, HW:PAYW],
                                    in1=GSL[:, PAYW + HW:2 * PAYW], op=ALU.add)
                    for i in range(2, NCORES):
                        e.tensor_tensor(out=GACC[:, 0:HW], in0=GACC[:, 0:HW],
                                        in1=GSL[:, i * PAYW:i * PAYW + HW], op=ALU.add)
                        e.tensor_tensor(out=GACC[:, HW:PAYW], in0=GACC[:, HW:PAYW],
                                        in1=GSL[:, i * PAYW + HW:(i + 1) * PAYW], op=ALU.add)
                    ga = GACC[:, 0:PAYW].rearrange("p (h c) -> p h c", c=65)
                    for hh in range(H):
                        e.reciprocal(out=GRC[:, hh:hh + 1], in_=ga[:, hh, 64:65])
                    for hh in range(H):
                        e.tensor_scalar(out=GN[:, 64 * hh:64 * (hh + 1)], in0=ga[:, hh, 0:64],
                                        scalar1=GRC[:, hh:hh + 1], scalar2=None,
                                        op0=ALU.mult).then_inc(s_vc.h, 1)
                        s_vc.inc()
                gn_done = s_vc.v

                if DEBUG:
                    @blk.sync
                    def _(e):
                        e.wait_ge(s_vc.h, gn_done)
                        e.dma_start(out=dbg["gcomb"][:, :], in_=GACC[:, 0:PAYW]).then_inc(s_st.h, 16)
                        s_st.inc(16)

                gtr = {}

                v0gt = s_vc.v

                @blk.tensor
                def _(e):
                    e.wait_ge(s_vc.h, gn_done)
                    for hh in range(H):
                        bank = 4 + hh % 2
                        if hh < 2:
                            marks.need(e, ("bank", bank))
                        else:
                            e.wait_ge(s_vc.h, v0gt + hh - 1)
                        e.transpose(P[bank][0:64, 0:128], GN[:, 64 * hh:64 * (hh + 1)],
                                    ident_sb[:]).then_inc(s_mm.h, 1)
                        gtr[hh] = (s_mm.inc(), bank)

                @blk.vector
                def _(e):
                    for hh in range(H):
                        v, bank = gtr[hh]
                        e.wait_ge(s_mm.h, v)
                        kcc = hh // 2
                        hb = 64 * (hh % 2)
                        for pos, fcol in ((0, 64), (1, 64 + 1984)):
                            xsl = XKV[hb:hb + 64,
                                      NKV * kcc + fcol:NKV * kcc + fcol + 64]
                            last = e.scalar_tensor_tensor(out=xsl,
                                                   in0=P[bank][0:64, 64 * pos:64 * pos + 64],
                                                   scalar=flags[hb:hb + 64, pos:pos + 1],
                                                   in1=xsl, op0=ALU.mult, op1=ALU.add)
                        last.then_inc(s_vc.h, 1)
                        s_vc.inc()
                        marks.set(("bank", bank), s_vc)
                xa_done = s_vc.v
                final_wait[0] = xa_done

                gc_scope.close()
                open_scopes.remove(gc_scope)

                if DEBUG:
                    @blk.sync
                    def _(e):
                        e.wait_ge(s_vc.h, xa_done)
                        for k in range(KC):
                            e.dma_start(out=dbg["xa"][128 * k:128 * (k + 1), :],
                                        in_=XKV[:, NKV * k + 64:NKV * k + 64 + TOK]).then_inc(s_st.h, 16)
                            s_st.inc(16)

                if STAGE < 3:
                    raise StopEmission
                ln2_pair([0, 5], xa_done)
                ln2_all = max(ln2_done.values())

                if DEBUG:
                    @blk.gpsimd
                    def _(e):
                        e.wait_ge(s_vc.h, ln2_all)
                        for k in range(KC):
                            e.dma_start(out=dbg["xln2"][128 * k:128 * (k + 1), :],
                                        in_=XLN[:, NKV * k + 64:NKV * k + 64 + TOK]).then_inc(s_st.h, 16)
                            s_st.inc(16)

                final_wait[0] = ln2_all
                if STAGE < 4:
                    raise StopEmission

                lg_edge = emit_route_logits([0, 3], [4, 5], xa_done)
                emit_route_fold([0, 3], lg_edge,
                                [ln2_done[0], ln2_done[1], ln2_done[4], ln2_done[5]])
                wbf_done = emit_route_t([0, 1, 2, 3, 12, 13, 14, 15], [4, 5])

                if DEBUG:
                    @blk.sync
                    def _(e):
                        e.wait_ge(s_vc.h, wbf_done)
                        e.dma_start(out=dbg["wroute"][:, :], in_=WTM[:]).then_inc(s_st.h, 16)
                        e.dma_start(out=dbg["lg"][:, :], in_=LG[:]).then_inc(s_st.h, 16)
                        e.dma_start(out=dbg["sct"][:, :], in_=SCT[:]).then_inc(s_st.h, 16)
                        s_st.inc(32)

                rt_scope.close()
                open_scopes.remove(rt_scope)
                ln2_scope.close()
                open_scopes.remove(ln2_scope)
                final_wait[0] = wbf_done

                if STAGE < 5:
                    raise StopEmission
                moe_scope = ExitStack()
                open_scopes.append(moe_scope)
                sbM = mk_sb(moe_scope)
                WGB = sbM("WGB", [128, 2 * 8 * 384], BF16)
                WUB = sbM("WUB", [128, 2 * 8 * 384], BF16)
                WDB = sbM("WDB", [128, IC_E * 1024], BF16)
                HND = sbM("HND", [128, 19 * 512], BF16)
                EHND = sbM("EHND", [128, 2 * 6 * 512], BF16)
                WBC = sbM("WBC", [128, 2 * 512], BF16)
                SGB = sbM("SGB", [128, 2 * 512], BF16)
                wgb3 = WGB[:].rearrange("p (b k c) -> p b k c", b=2, k=8)
                wub3 = WUB[:].rearrange("p (b k c) -> p b k c", b=2, k=8)
                wdb3 = WDB[:].rearrange("p (i c) -> p i c", c=1024)

                def icgroups(nic, first=3):
                    out = []
                    c = 0
                    while c < nic:
                        n = min(first if c == 0 else 3, nic - c)
                        out.append((c, n))
                        c += n
                    return out

                passes = []
                for ex in (-1, -2):
                    passes.append((ex, [0, 1]))
                    passes.append((ex, [2, 3]))
                for ex in range(8):
                    passes.append((ex, [0, 1]))
                    passes.append((ex, [2, 3]))
                if MOE_ONLY == 'shared':
                    passes = passes[:4]
                elif isinstance(MOE_ONLY, int):
                    passes = [p_ for p_ in passes if p_[0] == MOE_ONLY]

                gseq_box = [0]
                pending = [None]

                def hnd_ap(par, ntiles, ic, ti, g0):
                    if ic < g0:
                        sl = par * 6 + ic * ntiles + ti
                        return EHND[:, 512 * sl:512 * (sl + 1)]
                    sl = (ic - g0) * ntiles + ti
                    return HND[:, 512 * sl:512 * (sl + 1)]

                def emit_group(pid, ex, tiles, par, g_i, c0, n, wbc_done, hm_box, g0):
                    ntiles = len(tiles)
                    slot = gseq_box[0] % 2
                    gseq_cur = gseq_box[0]
                    gseq_box[0] += 1

                    @blk.gpsimd
                    def _(e):
                        if gseq_cur < 2:
                            e.wait_ge(s_vc.h, wbf_done)
                        marks.need(e, ("wgu", slot))
                        if ex < 0:
                            so = c0 + (0 if ex == -1 else IC_E)
                            sg = swg_d[:, 128 * so:128 * (so + n)]
                            su = swu_d[:, 128 * so:128 * (so + n)]
                        else:
                            sg = wg_d[ex, :, 128 * c0:128 * (c0 + n)]
                            su = wu_d[ex, :, 128 * c0:128 * (c0 + n)]
                        e.dma_start(out=wgb3[:, slot, :, 0:128 * n],
                                    in_=sg.rearrange("(k p) c -> p k c", p=128)).then_inc(s_w.h, 16)
                        s_w.inc(16)
                        e.dma_start(out=wub3[:, slot, :, 0:128 * n],
                                    in_=su.rearrange("(k p) c -> p k c", p=128)).then_inc(s_w.h, 16)
                        s_w.inc(16)
                    gu_w = s_w.v

                    gu_mm = {}
                    ssc0 = s_sc.v
                    vb0 = s_vc.v

                    def uwv(j):
                        return vb0 + (1 if j == 0 else 2 * j)

                    @blk.tensor
                    def _(e):
                        e.wait_ge(s_w.h, gu_w)
                        if pid == 0 and g_i == 0:
                            e.wait_ge(s_vc.h, ln2_all)
                        for ti, mt in enumerate(tiles):
                            tcol = 64 + 512 * mt
                            for ic in range(n):
                                j = ti * n + ic
                                gbank = j % 2
                                ubank = 2 + j % 2
                                if j < 2:
                                    marks.need(e, ("bank", gbank))
                                    marks.need(e, ("bank", ubank))
                                else:
                                    e.wait_ge(s_sc.h, ssc0 + j - 1)
                                    e.wait_ge(s_vc.h, uwv(j - 2))
                                last = None
                                for kc in range(KC):
                                    last = e.matmul(P[gbank][:, 0:512],
                                             wgb3[:, slot, kc, 128 * ic:128 * (ic + 1)],
                                             XLN[:, NKV * kc + tcol:NKV * kc + tcol + 512],
                                             start=(kc == 0), stop=(kc == KC - 1))
                                last.then_inc(s_mm.h, 1)
                                gv = s_mm.inc()
                                last = None
                                for kc in range(KC):
                                    last = e.matmul(P[ubank][:, 0:512],
                                             wub3[:, slot, kc, 128 * ic:128 * (ic + 1)],
                                             XLN[:, NKV * kc + tcol:NKV * kc + tcol + 512],
                                             start=(kc == 0), stop=(kc == KC - 1))
                                last.then_inc(s_mm.h, 1)
                                gu_mm[j] = (gv, s_mm.inc())
                        marks.set(("wgu", slot), s_mm)

                    nj = len(tiles) * n
                    sgd = {}

                    @blk.scalar
                    def _(e):
                        for j in range(nj):
                            gv, uv = gu_mm[j]
                            e.wait_ge(s_mm.h, gv)
                            if j < 2:
                                marks.need(e, ("sgb", j % 2))
                            else:
                                e.wait_ge(s_vc.h, vb0 + 2 * j - 1)
                            e.activation(out=SGB[:, 512 * (j % 2):512 * (j % 2 + 1)],
                                         in_=P[j % 2][:, 0:512],
                                         func=(AF.Identity if MOE_DIAG == 1 else AF.Silu)).then_inc(s_sc.h, 1)
                            sgd[j] = s_sc.inc()
                            marks.set(("bank", j % 2), s_sc)

                    @blk.vector
                    def _(e):
                        if ex >= 0 and g_i == 0 and wbc_done is not None:
                            e.wait_ge(s_vc.h, wbc_done)
                        if g_i == 0:
                            marks.need(e, ("ehnd", par))
                        if g_i == 1:
                            marks.need(e, ("hnd", 0))

                        def emit_h(j):
                            e.wait_ge(s_sc.h, sgd[j])
                            ti, ic = divmod(j, n)
                            ap = hnd_ap(par, len(tiles), c0 + ic, ti, g0)
                            if MOE_DIAG == 2:
                                e.tensor_scalar(out=ap, in0=ap, scalar1=1.0, scalar2=None,
                                                op0=ALU.mult).then_inc(s_vc.h, 1)
                            else:
                                e.tensor_tensor(out=ap,
                                                in0=SGB[:, 512 * (j % 2):512 * (j % 2 + 1)],
                                                in1=ap, op=ALU.mult).then_inc(s_vc.h, 1)
                            s_vc.inc()
                            marks.set(("sgb", j % 2), s_vc)

                        prev = None
                        for j in range(nj):
                            ti, ic = divmod(j, n)
                            gv, uv = gu_mm[j]
                            e.wait_ge(s_mm.h, uv)
                            ap = hnd_ap(par, len(tiles), c0 + ic, ti, g0)
                            if ex >= 0:
                                last = e.tensor_tensor(out=ap, in0=P[2 + j % 2][:, 0:512],
                                                       in1=WBC[:, 512 * ti:512 * (ti + 1)],
                                                       op=ALU.mult)
                            else:
                                last = e.tensor_copy(out=ap, in_=P[2 + j % 2][:, 0:512])
                            last.then_inc(s_vc.h, 1)
                            s_vc.inc()
                            marks.set(("bank", 2 + j % 2), s_vc)
                            if prev is not None:
                                emit_h(prev)
                            prev = j
                        emit_h(prev)
                    hm_box[0] = s_vc.v

                def emit_wd(ex, tiles, par, hm_last, g0):
                    for h0, hn in [(0, IC_E)]:
                        wd_ready2 = s_w.v

                        for ti, mt in enumerate(tiles):
                            tcol = 64 + 512 * mt
                            for oc in range(8):
                                @blk.tensor
                                def _(e):
                                    bank = 6 + oc % 2
                                    marks.need(e, ("bank", bank))
                                    e.wait_ge(s_w.h, wd_ready2)
                                    e.wait_ge(s_vc.h, hm_last)
                                    last = None
                                    for icl in range(hn):
                                        ic = h0 + icl
                                        last = e.matmul(P[bank][:, 0:512],
                                                 wdb3[:, icl, 128 * oc:128 * (oc + 1)],
                                                 hnd_ap(par, len(tiles), ic, ti, g0),
                                                 start=(icl == 0), stop=(icl == hn - 1))
                                    last.then_inc(s_mm.h, 1)
                                    R["ymm"] = s_mm.inc()

                                @blk.vector
                                def _(e):
                                    bank = 6 + oc % 2
                                    e.wait_ge(s_mm.h, R["ymm"])
                                    xsl = XKV[:, NKV * oc + tcol:NKV * oc + tcol + 512]
                                    e.tensor_tensor(out=xsl, in0=P[bank][:, 0:512], in1=xsl,
                                                    op=ALU.add).then_inc(s_vc.h, 1)
                                    s_vc.inc()
                                    marks.set(("bank", bank), s_vc)
                        marks.set("wdb", s_mm)
                    marks.set(("ehnd", par), s_mm)
                    marks.set(("hnd", 0), s_mm)

                for pid, (ex, tiles) in enumerate(passes):
                    nic = IC_E
                    gl = icgroups(nic, 3)
                    par = pid % 2
                    g0sz = gl[0][1]

                    wbc_done = None
                    if ex >= 0:
                        wbmm = {}

                        @blk.tensor
                        def _(e):
                            if pid == 4 or (pid == 0 and isinstance(MOE_ONLY, int)):
                                e.wait_ge(s_vc.h, wbf_done)
                                e.wait_ge(s_w.h, sel_w)
                            for ti, mt in enumerate(tiles):
                                bank = 4 + ti
                                marks.need(e, ("bank", bank))
                                e.matmul(P[bank][:, 0:512], sel_sb[:, 128 * ex:128 * (ex + 1)],
                                         w_bf[:, 512 * mt:512 * (mt + 1)],
                                         start=True, stop=True).then_inc(s_mm.h, 1)
                                wbmm[ti] = (s_mm.inc(), bank)

                        @blk.vector
                        def _(e):
                            marks.need(e, ("wbc", 0))
                            for ti in range(len(tiles)):
                                v, bank = wbmm[ti]
                                e.wait_ge(s_mm.h, v)
                                e.tensor_copy(out=WBC[:, 512 * ti:512 * (ti + 1)],
                                              in_=P[bank][:, 0:512]).then_inc(s_vc.h, 1)
                                s_vc.inc()
                                marks.set(("bank", bank), s_vc)
                        wbc_done = s_vc.v

                    hm_box = [None]
                    emit_group(pid, ex, tiles, par, 0, gl[0][0], gl[0][1], wbc_done, hm_box, g0sz)

                    if pending[0] is not None and MOE_DIAG != 3:
                        emit_wd(*pending[0])

                    for g_i in range(1, len(gl)):
                        c0, n = gl[g_i]
                        emit_group(pid, ex, tiles, par, g_i, c0, n, wbc_done, hm_box, g0sz)

                    @blk.gpsimd
                    def _(e):
                        if pid == 0:
                            e.wait_ge(s_vc.h, wbf_done)
                        marks.need(e, "wdb")
                        src = swd_d if ex < 0 else wd_d[ex]
                        wo = 0 if ex >= -1 else IC_E
                        for (c0, n) in icgroups(IC_E):
                            e.dma_start(out=wdb3[:, c0:c0 + n, :],
                                        in_=src[128 * (wo + c0):128 * (wo + c0 + n), :]
                                        .rearrange("(i p) c -> p i c", p=128)).then_inc(s_w.h, 16)
                            s_w.inc(16)

                    if ex >= 0 and len(gl):
                        marks.set(("wbc", 0), s_vc, hm_box[0])
                    pending[0] = (ex, tiles, par, hm_box[0], g0sz)

                if pending[0] is not None and MOE_DIAG != 3:
                    emit_wd(*pending[0])
                moe_all = s_vc.v
                moe_scope.close()
                open_scopes.remove(moe_scope)
                final_wait[0] = moe_all

            open_scopes = []
            try:
                emit_rest()
            except StopEmission:
                for sc_ in reversed(open_scopes):
                    sc_.close()

            @blk.sync
            def _(e):
                e.wait_ge(s_vc.h, final_wait[0])
                for k in range(KC):
                    e.dma_start(out=yout[128 * k:128 * (k + 1), :],
                                in_=XKV[:, NKV * k + 64:NKV * k + 64 + TOK]).then_inc(s_st.h, 16)
                    s_st.inc(16)
                e.wait_ge(s_st.h, s_st.v)


    return nc, dbg


def _host_inputs(inputs):
    x = np.asarray(inputs["L0_attn_hidden_states"])[0]
    S = x.shape[0]
    bq = np.asarray(inputs["L0_attn_bq"])
    bk = np.asarray(inputs["L0_attn_bk"])
    bv = np.asarray(inputs["L0_attn_bv"])
    gate = np.asarray(inputs["L0_ffn_gate_w"])

    wqT = np.ascontiguousarray(np.asarray(inputs["L0_attn_Wq"]).T)
    wkT = np.ascontiguousarray(np.asarray(inputs["L0_attn_Wk"]).T)
    wvT = np.ascontiguousarray(np.asarray(inputs["L0_attn_Wv"]).T)
    b3 = np.concatenate([bq.reshape(8, 128).T, bk.reshape(8, 128).T,
                         bv.reshape(8, 128).T], axis=1).astype(np.float32)
    bvr = np.ascontiguousarray(bv.reshape(1, 1024))
    gw8 = np.ascontiguousarray(gate.T.reshape(8, 128, 8).transpose(1, 0, 2).reshape(128, 64))
    identf = np.eye(128, dtype=np.float32)

    kcorn = np.zeros((2, 128), np.float32)
    kcorn[0, 64:128] = 1.0
    kcorn[1, 0:64] = 1.0
    qcorn = np.zeros((2, 5 * GQ), np.float32)
    for c in range(5):
        jA = 2 * c - 2
        if 0 <= jA < 8:
            qcorn[0, GQ * c + 64 * jA:GQ * c + 64 * (jA + 1)] = -30000.0
        jB = 2 * c + 1
        if 0 <= jB < 8:
            qcorn[1, GQ * c + 64 * jB:GQ * c + 64 * (jB + 1)] = -30000.0
    kfix = np.zeros((2, 128), np.float32)
    kfix[0, 0:64] = 1.0
    kfix[1, 64:128] = 1.0
    hfix = np.zeros((1, NKV), np.float32)
    hfix[0, 0:64] = 1.0
    hfix[0, 2112:2176] = 1.0
    sel = np.zeros((8, 8 * 128), np.float32)
    for e_ in range(8):
        sel[e_, 128 * e_:128 * (e_ + 1)] = 1.0

    shared = dict(wqT=wqT, wkT=wkT, wvT=wvT, b3=b3, bvr=bvr, gw8=gw8,
                  wg=np.ascontiguousarray(np.asarray(inputs["L0_ffn_wg"])),
                  wu=np.ascontiguousarray(np.asarray(inputs["L0_ffn_wu"])),
                  wd=np.ascontiguousarray(np.asarray(inputs["L0_ffn_wd"])),
                  swg=np.ascontiguousarray(np.asarray(inputs["L0_ffn_swg"])),
                  swu=np.ascontiguousarray(np.asarray(inputs["L0_ffn_swu"])),
                  swd=np.ascontiguousarray(np.asarray(inputs["L0_ffn_swd"])),
                  identf=identf, kcorn=kcorn, qcorn=qcorn, kfix=kfix,
                  hfix=hfix, sel=sel)

    in_maps = []
    for ci in range(NCORES):
        t0 = ci * TOK
        prev = x[t0 - 64:t0] if ci > 0 else x[0:64]
        nxt = x[t0 + TOK:t0 + TOK + 64] if ci < 7 else x[0:64]
        kvx = np.concatenate([prev, x[t0:t0 + TOK], nxt, x[0:64], x[S - 64:S]], 0)
        xin = np.ascontiguousarray(kvx.T)

        qfix = np.zeros((2, TOK), np.float32)
        if ci == 0:
            qfix[0, 64:128] = -30000.0
        if ci == 7:
            qfix[1, 1920:1984] = -30000.0
        zf = np.ones((1, TOK), np.float32)
        if ci == 0:
            zf[0, 0:64] = 0.0
        if ci == 7:
            zf[0, TOK - 64:] = 0.0
        fl = np.zeros((128, 2), np.float32)
        fl[:, 0] = 1.0 if ci == 0 else 0.0
        fl[:, 1] = 1.0 if ci == 7 else 0.0
        m = dict(shared)
        m.update(xin=xin, qfix=qfix, zflagi=zf, flags=fl,
                 qnegi=np.full((1, 128), -30000.0, np.float32))
        in_maps.append(m)
    return in_maps


_CACHE = {}


def kernel(**inputs):
    if "nc" not in _CACHE:
        _CACHE["nc"], _CACHE["dbg"] = build_program()
    nc = _CACHE["nc"]
    in_maps = _host_inputs(inputs)
    res = run_bass_kernel_spmd(nc, in_maps, core_ids=list(range(NCORES)))
    kernel_last_results[0] = res
    outs = [np.ascontiguousarray(res.results[ci]["y"].T) for ci in range(NCORES)]
    return np.concatenate(outs, 0)[None].astype(np.float32)


kernel_last_results = [None]


# version 17
# speedup vs baseline: 1.0084x; 1.0084x over previous
import numpy as np
from contextlib import ExitStack

import concourse.bass as bass
import concourse.mybir as mybir
from concourse.bass_utils import run_bass_kernel_spmd

F32 = mybir.dt.float32
BF16 = mybir.dt.bfloat16
AF = mybir.ActivationFunctionType
ALU = mybir.AluOpType
AX = mybir.AxisListType

DEBUG = globals().get("DEBUG", False)
STAGE = 5
ASUB = 9
PSUB = 9
MOE_ONLY = None
MOE_DIAG = 0
DEBUG2 = False

NCORES = 8
TOK = 2048
NKV = 2304
KC = 8
H = 16
GQ = 512
INTER = 1408
IC_E = 11
IC_S = 22
PAYW = H * 65
LIVE = [(0, 128), (0, 256), (128, 384), (256, 512), (384, 512)]


class Sem:
    def __init__(self, h):
        self.h = h
        self.v = 0

    def inc(self, n=1):
        self.v += n
        return self.v


class StopEmission(Exception):
    pass


class Marks:
    def __init__(self):
        self.m = {}

    def need(self, e, res):
        if res in self.m:
            sem, v = self.m.pop(res)
            e.wait_ge(sem.h, v)

    def set(self, res, sem, v=None):
        self.m[res] = (sem, sem.v if v is None else v)


def build_program():
    nc = bass.Bass("TRN2", target_bir_lowering=False, debug=False,
                   detect_race_conditions=False)

    xin = nc.dram_tensor("xin", [1024, NKV], F32, kind="ExternalInput").ap()
    wqT = nc.dram_tensor("wqT", [1024, 1024], F32, kind="ExternalInput").ap()
    wkT = nc.dram_tensor("wkT", [1024, 1024], F32, kind="ExternalInput").ap()
    wvT = nc.dram_tensor("wvT", [1024, 1024], F32, kind="ExternalInput").ap()
    b3_d = nc.dram_tensor("b3", [128, 24], F32, kind="ExternalInput").ap()
    bvr_d = nc.dram_tensor("bvr", [1, 1024], F32, kind="ExternalInput").ap()
    gw8_d = nc.dram_tensor("gw8", [128, 64], F32, kind="ExternalInput").ap()
    wg_d = nc.dram_tensor("wg", [8, 1024, INTER], F32, kind="ExternalInput").ap()
    wu_d = nc.dram_tensor("wu", [8, 1024, INTER], F32, kind="ExternalInput").ap()
    wd_d = nc.dram_tensor("wd", [8, INTER, 1024], F32, kind="ExternalInput").ap()
    swg_d = nc.dram_tensor("swg", [1024, 2816], F32, kind="ExternalInput").ap()
    swu_d = nc.dram_tensor("swu", [1024, 2816], F32, kind="ExternalInput").ap()
    swd_d = nc.dram_tensor("swd", [2816, 1024], F32, kind="ExternalInput").ap()
    identf_d = nc.dram_tensor("identf", [128, 128], F32, kind="ExternalInput").ap()
    kcorn_d = nc.dram_tensor("kcorn", [2, 128], F32, kind="ExternalInput").ap()
    qcorn_d = nc.dram_tensor("qcorn", [2, 5 * GQ], F32, kind="ExternalInput").ap()
    kfix_d = nc.dram_tensor("kfix", [2, 128], F32, kind="ExternalInput").ap()
    qfix_d = nc.dram_tensor("qfix", [2, TOK], F32, kind="ExternalInput").ap()
    hfix_d = nc.dram_tensor("hfix", [1, NKV], F32, kind="ExternalInput").ap()
    qneg_d = nc.dram_tensor("qnegi", [1, 128], F32, kind="ExternalInput").ap()
    zflag_d = nc.dram_tensor("zflagi", [1, TOK], F32, kind="ExternalInput").ap()
    flags_d = nc.dram_tensor("flags", [128, 2], F32, kind="ExternalInput").ap()
    sel_d = nc.dram_tensor("sel", [8, 8 * 128], F32, kind="ExternalInput").ap()
    yout = nc.dram_tensor("y", [1024, TOK], F32, kind="ExternalOutput").ap()
    cc_in = nc.dram_tensor("cc_in", [128, PAYW], F32)
    cc_out = nc.dram_tensor("cc_out", [128 * NCORES, PAYW], F32,
                            addr_space="Shared")

    dbg = {}
    if DEBUG:
        def dbgt(name, shape):
            dbg[name] = nc.dram_tensor("dbg_" + name, shape, F32, kind="ExternalOutput").ap()
        dbgt("xln1", [1024, NKV])
        dbgt("qkv0", [128, 2 * TOK + 2 * NKV + 18 * 260])
        dbgt("xa", [1024, TOK])
        dbgt("pay", [128, PAYW])
        dbgt("gcomb", [128, PAYW])
        dbgt("wroute", [128, 16 * 8])
        dbgt("lg", [8, TOK])
        dbgt("sct", [128, 16 * 8])
        dbgt("xln2", [1024, TOK])
    if DEBUG2:
        dbg["norm0"] = nc.dram_tensor("dbg_norm0", [128, 3 * 1024 + 512], F32, kind="ExternalOutput").ap()

    es = ExitStack()

    def mk_sb(scope):
        def f(name, shape, dt):
            return scope.enter_context(nc.sbuf_tensor(name, shape, dt))
        return f

    sb = mk_sb(es)
    R = {}
    with es:
        XKV = sb("XKV", [128, KC * NKV], F32)
        XLN = sb("XLN", [128, KC * NKV], BF16)
        ident_sb = sb("ident_sb", [128, 128], F32)
        ones128 = sb("ones128", [128, 128], BF16)
        onesrow = sb("onesrow", [1, 128], BF16)
        onesrow_f = sb("onesrow_f", [1, 64], F32)
        onescol_f = sb("onescol_f", [128, 1], F32)
        b3_sb = sb("b3_sb", [128, 24], F32)
        bsc = sb("bsc", [128, 8], F32)
        gw_sb = sb("gw_sb", [128, 64], F32)
        WQs = sb("WQs", [128, 8 * 256], BF16)
        WKs = sb("WKs", [128, 8 * 256], BF16)
        flags = sb("flags_sb", [128, 2], F32)
        sel_sb = sb("sel_sb", [8, 8 * 128], BF16)
        w_bf = sb("w_bf", [8, TOK], BF16)
        eps_sb = sb("eps_sb", [128, 1], F32)

        P = [es.enter_context(nc.psum_tensor(f"P{i}", [128, 512], F32)) for i in range(8)]

        s_in = Sem(es.enter_context(nc.semaphore("s_in")))
        s_w = Sem(es.enter_context(nc.semaphore("s_w")))
        s_mm = Sem(es.enter_context(nc.semaphore("s_mm")))
        s_sc = Sem(es.enter_context(nc.semaphore("s_sc")))
        s_vc = Sem(es.enter_context(nc.semaphore("s_vc")))
        s_cc = Sem(es.enter_context(nc.semaphore("s_cc")))
        s_st = Sem(es.enter_context(nc.semaphore("s_st")))

        marks = Marks()

        with nc.Block() as blk:
            @blk.sync
            def _(e):
                for k in range(KC):
                    e.dma_start(out=XKV[:, NKV * k:NKV * (k + 1)],
                                in_=xin[128 * k:128 * (k + 1), :]).then_inc(s_in.h, 16)
                    s_in.inc(16)
                for dst, src in ((ident_sb, identf_d), (b3_sb, b3_d), (gw_sb, gw8_d),
                                 (flags, flags_d)):
                    e.dma_start(out=dst[:], in_=src[:, :]).then_inc(s_in.h, 16)
                    s_in.inc(16)
            in_base = s_in.v

            @blk.gpsimd
            def _(e):
                e.dma_start(out=sel_sb[:], in_=sel_d[:, :]).then_inc(s_w.h, 16)
                s_w.inc(16)
            sel_w = s_w.v

            @blk.vector
            def _(e):
                e.memset(ones128[:], 1.0)
                e.memset(onesrow[:], 1.0)
                e.memset(onesrow_f[:], 1.0)
                e.memset(onescol_f[:], 1.0)
                e.memset(eps_sb[:], 1e-5).then_inc(s_vc.h, 1)
                s_vc.inc()
            ones_ready = s_vc.v

            @blk.scalar
            def _(e):
                e.wait_ge(s_in.h, in_base)
                e.mul(out=bsc[:], in_=b3_sb[:, 0:8], mul=0.125).then_inc(s_sc.h, 1)
                s_sc.inc()
            bsc_ready = s_sc.v

            def ln_setup(scope_sb, col0, ncols, tag, slices=None):
                if slices is None:
                    slices = []
                    c = 0
                    while c < ncols:
                        w = min(512, ncols - c)
                        slices.append((col0 + c, w))
                        c += w
                inv = scope_sb(f"inv_{tag}", [128, ncols], F32)
                mean = scope_sb(f"mean_{tag}", [128, ncols], F32)
                sqb = scope_sb(f"sq_{tag}", [128, 2 * KC * 512], BF16)
                t1b = scope_sb(f"t1_{tag}", [128, 2 * 512], F32)
                var_b = scope_sb(f"var_{tag}", [128, 2 * 512], F32)
                sd_b = scope_sb(f"sd_{tag}", [128, 2 * 512], F32)
                tmpb = scope_sb(f"tmp_{tag}", [128, 2 * 512], F32)
                apply_done = {}

                def emit_pair(pr, wait_vc):
                    sq_ready = {}
                    stat_done = {}
                    fv = {}
                    sd_done = {}

                    @blk.scalar
                    def _(e):
                        e.wait_ge(s_in.h, in_base)
                        if wait_vc:
                            e.wait_ge(s_vc.h, wait_vc)
                        for ns in pr:
                            c0, w = slices[ns]
                            for k in range(KC):
                                e.copy(out=XLN[:, NKV * k + c0:NKV * k + c0 + w],
                                       in_=XKV[:, NKV * k + c0:NKV * k + c0 + w])
                        for kc in range(KC):
                            for ns in pr:
                                c0, w = slices[ns]
                                buf = kc * 2 + (ns % 2)
                                e.activation(out=sqb[:, 512 * buf:512 * buf + w],
                                             in_=XLN[:, NKV * kc + c0:NKV * kc + c0 + w],
                                             func=AF.Square).then_inc(s_sc.h, 1)
                                sq_ready[(ns, kc)] = s_sc.inc()

                    @blk.tensor
                    def _(e):
                        e.wait_ge(s_vc.h, ones_ready)
                        for ns in pr:
                            marks.need(e, ("bank", 2 * (ns % 2)))
                            marks.need(e, ("bank", 2 * (ns % 2) + 1))
                        last = None
                        for kc in range(KC):
                            for ns in pr:
                                c0, w = slices[ns]
                                buf = kc * 2 + (ns % 2)
                                e.wait_ge(s_sc.h, sq_ready[(ns, kc)])
                                e.matmul(P[2 * (ns % 2)][:, 0:w], ones128[:],
                                         XLN[:, NKV * kc + c0:NKV * kc + c0 + w],
                                         start=(kc == 0), stop=(kc == KC - 1))
                                last = e.matmul(P[2 * (ns % 2) + 1][:, 0:w], ones128[:],
                                         sqb[:, 512 * buf:512 * buf + w],
                                         start=(kc == 0), stop=(kc == KC - 1))
                        last.then_inc(s_mm.h, 1)
                        s_mm.inc()
                        for ns in pr:
                            stat_done[ns] = s_mm.v

                    @blk.vector
                    def _(e):
                        for ns in pr:
                            e.wait_ge(s_mm.h, stat_done[ns])
                        for ns in pr:
                            c0, w = slices[ns]
                            lc = c0 - col0
                            e.tensor_scalar(out=mean[:, lc:lc + w], in0=P[2 * (ns % 2)][:, 0:w],
                                            scalar1=1.0 / 1024.0, scalar2=None, op0=ALU.mult)
                        if len(pr) == 1:
                            e.drain()
                        for ns in pr:
                            c0, w = slices[ns]
                            lc = c0 - col0
                            e.tensor_tensor(out=t1b[:, 512 * (ns % 2):512 * (ns % 2) + w],
                                            in0=P[2 * (ns % 2)][:, 0:w],
                                            in1=mean[:, lc:lc + w], op=ALU.mult)
                        if len(pr) == 1:
                            e.drain()
                        for ns in pr:
                            c0, w = slices[ns]
                            e.tensor_tensor(out=var_b[:, 512 * (ns % 2):512 * (ns % 2) + w],
                                            in0=P[2 * (ns % 2) + 1][:, 0:w],
                                            in1=t1b[:, 512 * (ns % 2):512 * (ns % 2) + w],
                                            op=ALU.subtract).then_inc(s_vc.h, 1)
                            fv[ns] = s_vc.inc()
                            marks.set(("bank", 2 * (ns % 2)), s_vc)
                            marks.set(("bank", 2 * (ns % 2) + 1), s_vc)

                    @blk.scalar
                    def _(e):
                        for ns in pr:
                            e.wait_ge(s_vc.h, fv[ns])
                            e.activation(out=sd_b[:, 512 * (ns % 2):512 * (ns % 2) + slices[ns][1]],
                                         in_=var_b[:, 512 * (ns % 2):512 * (ns % 2) + slices[ns][1]],
                                         func=AF.Sqrt, bias=eps_sb[:, 0:1], scale=1.0 / 1024.0).then_inc(s_sc.h, 1)
                            sd_done[ns] = s_sc.inc()

                    @blk.vector
                    def _(e):
                        for ns in pr:
                            e.wait_ge(s_sc.h, sd_done[ns])
                        for ns in pr:
                            c0, w = slices[ns]
                            lc = c0 - col0
                            e.reciprocal(out=inv[:, lc:lc + w],
                                         in_=sd_b[:, 512 * (ns % 2):512 * (ns % 2) + w])
                        for ns in pr:
                            c0, w = slices[ns]
                            lc = c0 - col0
                            prev = None
                            for kc in range(KC):
                                e.tensor_tensor(out=tmpb[:, 512 * (kc % 2):512 * (kc % 2) + w],
                                                in0=XKV[:, NKV * kc + c0:NKV * kc + c0 + w],
                                                in1=mean[:, lc:lc + w], op=ALU.subtract)
                                if prev is not None:
                                    e.tensor_tensor(out=XLN[:, NKV * prev + c0:NKV * prev + c0 + w],
                                                    in0=tmpb[:, 512 * (prev % 2):512 * (prev % 2) + w],
                                                    in1=inv[:, lc:lc + w], op=ALU.mult)
                                prev = kc
                            e.tensor_tensor(out=XLN[:, NKV * prev + c0:NKV * prev + c0 + w],
                                            in0=tmpb[:, 512 * (prev % 2):512 * (prev % 2) + w],
                                            in1=inv[:, lc:lc + w], op=ALU.mult).then_inc(s_vc.h, 1)
                            apply_done[ns] = s_vc.inc()
                return inv, mean, apply_done, emit_pair, len(slices)

            ln1_scope = ExitStack()
            inv1, mean1, ln1_done, ln1_pair, ln1_ns = ln_setup(mk_sb(ln1_scope), 0, NKV, "ln1")
            for pair0 in range(0, ln1_ns, 2):
                ln1_pair([q for q in (pair0, pair0 + 1) if q < ln1_ns], None)
            ln1_all = max(ln1_done.values())

            if DEBUG:
                @blk.gpsimd
                def _(e):
                    e.wait_ge(s_vc.h, ln1_all)
                    for k in range(KC):
                        e.dma_start(out=dbg["xln1"][128 * k:128 * (k + 1), :],
                                    in_=XLN[:, NKV * k:NKV * (k + 1)]).then_inc(s_st.h, 16)
                        s_st.inc(16)

            ln1_scope.close()

            final_wait = [ln1_all]

            def emit_rest():
                if STAGE < 2:
                    raise StopEmission
                attn_scope = ExitStack()
                open_scopes.append(attn_scope)
                sbA = mk_sb(attn_scope)
                WVs = sbA("WVs", [128, 8 * 256], BF16)
                QSB = sbA("QSB", [128, 2 * TOK], BF16)
                KSB = sbA("KSB", [128, 2 * NKV], BF16)
                VSB = sbA("VSB", [128, 18 * 260], BF16)
                QG = sbA("QG", [128, 2 * 128], BF16)
                EB = sbA("EB", [128, 12 * 512], BF16)
                EG = sbA("EG", [128, 17 * 128], BF16)
                CTU = sbA("CTU", [128, 4 * 512], BF16)
                RR = sbA("RR", [1, 4 * 512], BF16)
                TPH = sbA("TPH", [128, 128], F32)
                payload = sbA("payload", [128, PAYW], F32)
                kcorn = sbA("kcorn_sb", [2, 128], BF16)
                qcorn = sbA("qcorn_sb", [2, 5 * GQ], BF16)
                kfix = sbA("kfix_sb", [2, 128], BF16)
                qfix = sbA("qfix_sb", [2, TOK], BF16)
                hfix = sbA("hfix_sb", [1, NKV], BF16)
                qneg = sbA("qneg_sb", [1, 128], BF16)
                zflag = sbA("zflag_sb", [1, TOK], BF16)
                bvr_sb = sbA("bvr_sb", [1, 1024], BF16)
                wq3 = WQs[:].rearrange("p (k c) -> p k c", c=256)
                wk3 = WKs[:].rearrange("p (k c) -> p k c", c=256)
                wv3 = WVs[:].rearrange("p (k c) -> p k c", c=256)
                vsb4 = VSB[:].rearrange("p (t h c) -> p t h c", h=4, c=65)
                qc3 = qcorn[:].rearrange("p (c q) -> p c q", q=GQ)

                aconst_w = [None]

                @blk.vector
                def _(e):
                    e.memset(vsb4[:, :, :, 64:65], 1.0).then_inc(s_vc.h, 1)
                    s_vc.inc()
                vones = s_vc.v

                pay_last = [None]
                track_mid = [0]
                v_evac_last = [0]
                pend_norm = [None]

                def emit_norm_finish(rb, hb, sl0, gp, oc, qd):
                    rbm = {}

                    @blk.tensor
                    def _(e):
                        for gi in range(2):
                            bank = 6 + gi
                            marks.need(e, ("bank", bank))
                            e.wait_ge(s_vc.h, rb[gi])
                            e.matmul(P[bank][hb:hb + 64, 0:512], onesrow[0:1, 0:64],
                                     RR[0:1, 512 * (sl0 + gi):512 * (sl0 + gi) + 512],
                                     start=True, stop=True).then_inc(s_mm.h, 1)
                            rbm[gi] = s_mm.inc()

                    @blk.vector
                    def _(e):
                        with nc.allow_low_precision(reason="bf16 ctx staging; residual add stays fp32"):
                            for gi in range(2):
                                e.wait_ge(s_mm.h, rbm[gi])
                                e.tensor_tensor(out=CTU[hb:hb + 64, 512 * (sl0 + gi):512 * (sl0 + gi + 1)],
                                                in0=CTU[hb:hb + 64, 512 * (sl0 + gi):512 * (sl0 + gi + 1)],
                                                in1=P[6 + gi][hb:hb + 64, 0:512], op=ALU.mult)
                        for gi in range(2):
                            g = 2 * gp + gi
                            xsl = XKV[hb:hb + 64,
                                      NKV * (2 * qd + oc) + 64 + GQ * g:NKV * (2 * qd + oc) + 64 + GQ * (g + 1)]
                            e.tensor_tensor(out=xsl, in0=CTU[hb:hb + 64, 512 * (sl0 + gi):512 * (sl0 + gi + 1)],
                                            in1=xsl, op=ALU.add).then_inc(s_vc.h, 1)
                            s_vc.inc()
                            marks.set(("bank", 6 + gi), s_vc)

                for qd in range(4):
                    if PSUB < 1:
                        break
                    @blk.gpsimd
                    def _(e):
                        c0 = 256 * qd
                        marks.need(e, "wslice")
                        for buf3, src in ((wq3, wqT), (wk3, wkT)):
                            e.dma_start(out=buf3,
                                        in_=src[:, c0:c0 + 256].rearrange("(k p) c -> p k c", p=128)
                                        ).then_inc(s_w.h, 16)
                            s_w.inc(16)
                    wqk_ready = s_w.v

                    @blk.gpsimd
                    def _(e):
                        c0 = 256 * qd
                        if qd == 0:
                            e.wait_ge(s_vc.h, ln1_all)
                        e.dma_start(out=wv3,
                                    in_=wvT[:, c0:c0 + 256].rearrange("(k p) c -> p k c", p=128)
                                    ).then_inc(s_w.h, 16)
                        s_w.inc(16)
                        if qd == 0:
                            for dst, src in ((kcorn, kcorn_d), (qcorn, qcorn_d), (kfix, kfix_d),
                                             (qfix, qfix_d), (hfix, hfix_d), (zflag, zflag_d),
                                             (bvr_sb, bvr_d), (qneg, qneg_d)):
                                e.dma_start(out=dst[:], in_=src[:, :]).then_inc(s_w.h, 16)
                                s_w.inc(16)
                            aconst_w[0] = s_w.v
                    wv_ready = s_w.v

                    proj = {}

                    sc0 = s_sc.v
                    vv0 = s_vc.v

                    @blk.tensor
                    def _(e):
                        if PSUB == 11:
                            return
                        e.wait_ge(s_w.h, wqk_ready)
                        if qd == 0:
                            e.wait_ge(s_sc.h, bsc_ready)
                        marks.need(e, "qsb")
                        marks.need(e, "ksb")
                        marks.need(e, "vsb")
                        for i in range(8):
                            oc, ts_ = divmod(i, 4)
                            bank = 2 + i % 2
                            if qd == 0:
                                e.wait_ge(s_vc.h, ln1_done[min(ts_ + 1, 4)])
                            if i < 2:
                                marks.need(e, ("bank", bank))
                            else:
                                e.wait_ge(s_sc.h, sc0 + i - 1)
                            last = None
                            for kc in range(KC):
                                last = e.matmul(P[bank][:, 0:512],
                                         wq3[:, kc, 128 * oc:128 * (oc + 1)],
                                         XLN[:, NKV * kc + 64 + 512 * ts_:NKV * kc + 64 + 512 * (ts_ + 1)],
                                         start=(kc == 0), stop=(kc == KC - 1))
                            last.then_inc(s_mm.h, 1)
                            proj[("q", i)] = (s_mm.inc(), bank)
                        for i in range(10):
                            oc, ts_ = divmod(i, 5)
                            c0 = 512 * ts_
                            w = min(512, NKV - c0)
                            bank = 2 + i % 2
                            if qd == 0:
                                e.wait_ge(s_vc.h, ln1_done[ts_])
                            e.wait_ge(s_sc.h, sc0 + 8 + i - 1)
                            last = None
                            for kc in range(KC):
                                last = e.matmul(P[bank][:, 0:w],
                                         wk3[:, kc, 128 * oc:128 * (oc + 1)],
                                         XLN[:, NKV * kc + c0:NKV * kc + c0 + w],
                                         start=(kc == 0), stop=(kc == KC - 1))
                            last.then_inc(s_mm.h, 1)
                            proj[("k", i)] = (s_mm.inc(), bank)
                        e.wait_ge(s_w.h, wv_ready)
                        if qd == 0:
                            e.wait_ge(s_w.h, aconst_w[0])
                            e.wait_ge(s_vc.h, vones)
                        for tc in range(18):
                            bank = tc % 2
                            if tc < 2:
                                marks.need(e, ("bank", bank))
                            else:
                                e.wait_ge(s_vc.h, vv0 + tc - 1)
                            for kc in range(KC):
                                e.matmul(P[bank][:, 0:256],
                                         XLN[:, NKV * kc + 128 * tc:NKV * kc + 128 * (tc + 1)],
                                         wv3[:, kc, :],
                                         start=(kc == 0), stop=False)
                            e.matmul(P[bank][:, 0:256], onesrow[:],
                                     bvr_sb[0:1, 256 * qd:256 * (qd + 1)],
                                     start=False, stop=True).then_inc(s_mm.h, 1)
                            proj[("v", tc)] = (s_mm.inc(), bank)
                        for oc in range(2):
                            bank = 2 + oc % 2
                            e.wait_ge(s_sc.h, sc0 + 18 + oc - 1)
                            last = None
                            for kc in range(KC):
                                last = e.matmul(P[bank][:, 0:128],
                                         wq3[:, kc, 128 * oc:128 * (oc + 1)],
                                         XLN[:, NKV * kc + 2176:NKV * kc + 2304],
                                         start=(kc == 0), stop=(kc == KC - 1))
                            last.then_inc(s_mm.h, 1)
                            proj[("qg", oc)] = (s_mm.inc(), bank)
                        marks.set("wslice", s_mm)

                    @blk.scalar
                    def _(e):
                        for i in range(8):
                            oc, ts_ = divmod(i, 4)
                            v, bank = proj[("q", i)]
                            e.wait_ge(s_mm.h, v)
                            e.activation(out=QSB[:, TOK * oc + 512 * ts_:TOK * oc + 512 * (ts_ + 1)],
                                         in_=P[bank][:, 0:512], func=AF.Identity,
                                         bias=bsc[:, 2 * qd + oc:2 * qd + oc + 1],
                                         scale=0.125).then_inc(s_sc.h, 1)
                            s_sc.inc()
                            marks.set(("bank", bank), s_sc)
                        for i in range(10):
                            oc, ts_ = divmod(i, 5)
                            c0 = 512 * ts_
                            w = min(512, NKV - c0)
                            v, bank = proj[("k", i)]
                            e.wait_ge(s_mm.h, v)
                            e.activation(out=KSB[:, NKV * oc + c0:NKV * oc + c0 + w],
                                         in_=P[bank][:, 0:w], func=AF.Identity,
                                         bias=b3_sb[:, 8 + 2 * qd + oc:8 + 2 * qd + oc + 1],
                                         scale=1.0).then_inc(s_sc.h, 1)
                            s_sc.inc()
                            marks.set(("bank", bank), s_sc)
                        for oc in range(2):
                            v, bank = proj[("qg", oc)]
                            e.wait_ge(s_mm.h, v)
                            e.activation(out=QG[:, 128 * oc:128 * (oc + 1)],
                                         in_=P[bank][:, 0:128], func=AF.Identity,
                                         bias=bsc[:, 2 * qd + oc:2 * qd + oc + 1],
                                         scale=0.125).then_inc(s_sc.h, 1)
                            s_sc.inc()
                            marks.set(("bank", bank), s_sc)
                    qk_evac = s_sc.v

                    @blk.vector
                    def _(e):
                        for tc in range(18):
                            v, bank = proj[("v", tc)]
                            e.wait_ge(s_mm.h, v)
                            e.tensor_copy(out=vsb4[:, tc, :, 0:64],
                                          in_=P[bank][:, 0:256].rearrange("p (h c) -> p h c", c=64)
                                          ).then_inc(s_vc.h, 1)
                            s_vc.inc()
                            marks.set(("bank", bank), s_vc)
                    v_evac = s_vc.v
                    v_evac_last[0] = v_evac

                    if DEBUG and qd == 0:
                        @blk.gpsimd
                        def _(e):
                            e.wait_ge(s_sc.h, qk_evac)
                            e.wait_ge(s_vc.h, v_evac)
                            e.dma_start(out=dbg["qkv0"][:, 0:2 * TOK], in_=QSB[:]).then_inc(s_st.h, 16)
                            e.dma_start(out=dbg["qkv0"][:, 2 * TOK:2 * TOK + 2 * NKV], in_=KSB[:]).then_inc(s_st.h, 16)
                            e.dma_start(out=dbg["qkv0"][:, 2 * TOK + 2 * NKV:], in_=VSB[:]).then_inc(s_st.h, 16)
                            s_st.inc(48)

                    for hl in range(4):
                        if ASUB < 2:
                            continue
                        h = 4 * qd + hl
                        oc, half = divmod(hl, 2)
                        Qrow = QSB[64 * half:64 * half + 64, TOK * oc:TOK * (oc + 1)]
                        Krow = KSB[64 * half:64 * half + 64, NKV * oc:NKV * (oc + 1)]
                        QGrow = QG[64 * half:64 * half + 64, 128 * oc:128 * (oc + 1)]

                        for gp in range(2):
                            ginfo = {}
                            smarks = {0: {}, 1: {}}
                            emarks = {0: {}, 1: {}}
                            sc00 = s_sc.v
                            for gi in range(2):
                                g = 2 * gp + gi
                                par = (h * 4 + g) % 2
                                smark = smarks[gi]

                                @blk.tensor
                                def _(e, g=g, gi=gi, smark=smark):
                                    e.wait_ge(s_sc.h, qk_evac)
                                    e.wait_ge(s_vc.h, v_evac)
                                    for c in range(6):
                                        l = gi * 6 + c
                                        sbank = (0, 1, 4, 5)[l % 4]
                                        if l < 4:
                                            marks.need(e, ("bank", sbank))
                                        else:
                                            e.wait_ge(s_sc.h, sc00 + l - 3)
                                        if c == 0:
                                            lo, hi = 0, 512
                                            e.matmul(P[sbank][:, lo:hi], Krow[:, 2176:2304],
                                                     Qrow[:, GQ * g + lo:GQ * g + hi], start=True, stop=False)
                                            e.matmul(P[sbank][:, lo:hi], kfix[:],
                                                     qfix[:, GQ * g + lo:GQ * g + hi],
                                                     start=False, stop=True).then_inc(s_mm.h, 1)
                                        else:
                                            cw = c - 1
                                            k0 = GQ * g + 128 * cw
                                            lo, hi = LIVE[cw]
                                            e.matmul(P[sbank][:, lo:hi], Krow[:, k0:k0 + 128],
                                                     Qrow[:, GQ * g + lo:GQ * g + hi], start=True, stop=False)
                                            e.matmul(P[sbank][:, lo:hi], kcorn[:],
                                                     qc3[:, cw, lo:hi],
                                                     start=False, stop=True).then_inc(s_mm.h, 1)
                                        smark[c] = (s_mm.inc(), sbank)

                            for gi in range(2):
                                g = 2 * gp + gi
                                par = (h * 4 + g) % 2
                                smark = smarks[gi]
                                emark = emarks[gi]

                                @blk.scalar
                                def _(e, par=par, smark=smark, emark=emark):
                                    marks.need(e, ("eb", par))
                                    for c in range(6):
                                        v, sbank = smark[c]
                                        lo, hi = (0, 512) if c == 0 else LIVE[c - 1]
                                        e.wait_ge(s_mm.h, v)
                                        e.activation(out=EB[:, 512 * (6 * par + c) + lo:512 * (6 * par + c) + hi],
                                                     in_=P[sbank][:, lo:hi], func=AF.Exp).then_inc(s_sc.h, 1)
                                        emark[c] = s_sc.inc()
                                        marks.set(("bank", sbank), s_sc)

                            for gi in range(2):
                                g = 2 * gp + gi
                                pcb = 2 + gi
                                par = (h * 4 + g) % 2
                                emark = emarks[gi]

                                @blk.tensor
                                def _(e, g=g, gi=gi, pcb=pcb, par=par, emark=emark, hl=hl):
                                    marks.need(e, ("bank", pcb))
                                    for c in range(6):
                                        lo, hi = (0, 512) if c == 0 else LIVE[c - 1]
                                        tc = 17 if c == 0 else 4 * g + (c - 1)
                                        e.wait_ge(s_sc.h, emark[c])
                                        e.matmul(P[pcb][0:65, lo:hi],
                                                 vsb4[:, tc, hl, :],
                                                 EB[:, 512 * (6 * par + c) + lo:512 * (6 * par + c) + hi],
                                                 start=(c == 0), stop=(c == 5)).then_inc(s_mm.h, 1)
                                        s_mm.inc()
                                    ginfo[gi] = (s_mm.v, pcb)
                                    marks.set(("eb", par), s_mm)

                            rb = {}
                            hb = 64 * half
                            sl0 = 2 * ((h * 2 + gp) % 2)

                            @blk.vector
                            def _(e, rb=rb, hb=hb, sl0=sl0, gp=gp):
                                for gi in range(2):
                                    v, pcb = ginfo[gi]
                                    e.wait_ge(s_mm.h, v)
                                with nc.allow_low_precision(reason="bf16 softmax recip staging"):
                                    for gi in range(2):
                                        _, pcb = ginfo[gi]
                                        e.reciprocal(out=RR[:, 512 * (sl0 + gi):512 * (sl0 + gi) + 512],
                                                     in_=P[pcb][64:65, 0:512])
                                for gi in range(2):
                                    _, pcb = ginfo[gi]
                                    e.tensor_copy(out=CTU[hb:hb + 64, 512 * (sl0 + gi):512 * (sl0 + gi + 1)],
                                                  in_=P[pcb][0:64, 0:512])
                                with nc.allow_low_precision(reason="bf16 softmax recip staging; fp32 accum in PSUM"):
                                    for gi in range(2):
                                        g = 2 * gp + gi
                                        e.tensor_tensor(out=RR[:, 512 * (sl0 + gi):512 * (sl0 + gi) + 512],
                                                        in0=RR[:, 512 * (sl0 + gi):512 * (sl0 + gi) + 512],
                                                        in1=zflag[:, GQ * g:GQ * (g + 1)],
                                                        op=ALU.mult).then_inc(s_vc.h, 1)
                                        rb[gi] = s_vc.inc()
                                for gi in range(2):
                                    _, pcb = ginfo[gi]
                                    marks.set(("bank", pcb), s_vc, rb[gi])

                            if pend_norm[0] is not None:
                                emit_norm_finish(*pend_norm[0])
                            pend_norm[0] = (rb, hb, sl0, gp, oc, qd)


                    if ASUB == 2:
                        track_mid[0] = s_vc.v
                    for hl in range(4):
                        if ASUB < 3:
                            break
                        h = 4 * qd + hl
                        oc, half = divmod(hl, 2)
                        Krow = KSB[64 * half:64 * half + 64, NKV * oc:NKV * (oc + 1)]
                        QGrow = QG[64 * half:64 * half + 64, 128 * oc:128 * (oc + 1)]
                        gs = {}

                        if hl == 1 and pend_norm[0] is not None:
                            emit_norm_finish(*pend_norm[0])
                            pend_norm[0] = None
                        scg0 = s_sc.v

                        @blk.tensor
                        def _(e, scg0=scg0):
                            for r in range(5):
                                bank = 4 + r % 2
                                if r < 2:
                                    marks.need(e, ("bank", bank))
                                else:
                                    e.wait_ge(s_sc.h, scg0 + r - 1)
                                nch = 4 if r < 4 else 1
                                last = None
                                for cc in range(nch):
                                    kchunk = 4 * r + cc
                                    k0 = 128 * kchunk
                                    last = e.matmul(P[bank][:, 128 * cc:128 * (cc + 1)],
                                             Krow[:, k0:k0 + 128], QGrow,
                                             start=True, stop=(kchunk not in (0, 16)))
                                    if kchunk in (0, 16):
                                        last = e.matmul(P[bank][:, 128 * cc:128 * (cc + 1)],
                                                 hfix[0:1, k0:k0 + 128], qneg[:],
                                                 start=False, stop=True)
                                last.then_inc(s_mm.h, 1)
                                gs[r] = (s_mm.inc(), bank)

                        ge_ = {}

                        @blk.scalar
                        def _(e):
                            marks.need(e, ("eg", 0))
                            for r in range(5):
                                v, bank = gs[r]
                                w = 512 if r < 4 else 128
                                e.wait_ge(s_mm.h, v)
                                e.activation(out=EG[:, 512 * r:512 * r + w],
                                             in_=P[bank][:, 0:w], func=AF.Exp).then_inc(s_sc.h, 1)
                                ge_[r] = s_sc.inc()
                                marks.set(("bank", bank), s_sc)

                        @blk.tensor
                        def _(e, hl=hl):
                            marks.need(e, ("bank", 2))
                            e.wait_ge(s_sc.h, ge_[4])
                            last = None
                            for kchunk in range(17):
                                last = e.matmul(P[2][0:65, 0:128], vsb4[:, kchunk, hl, :],
                                         EG[:, 128 * kchunk:128 * (kchunk + 1)],
                                         start=(kchunk == 0), stop=(kchunk == 16))
                            last.then_inc(s_mm.h, 1)
                            R["pcg"] = s_mm.inc()
                            marks.set(("eg", 0), s_mm)

                        @blk.vector
                        def _(e):
                            e.wait_ge(s_mm.h, R["pcg"])
                            marks.need(e, ("tph", 0))
                            e.tensor_copy(out=TPH[0:65, :], in_=P[2][0:65, 0:128]).then_inc(s_vc.h, 1)
                            R["tph"] = s_vc.inc()
                            marks.set(("bank", 2), s_vc)

                        @blk.tensor
                        def _(e):
                            marks.need(e, ("bank", 3))
                            e.wait_ge(s_vc.h, R["tph"])
                            e.transpose(P[3][:, 0:65], TPH[0:65, :], ident_sb[0:65, 0:65]).then_inc(s_mm.h, 1)
                            R["trm"] = s_mm.inc()
                            marks.set(("tph", 0), s_mm)

                        @blk.vector
                        def _(e, h=h):
                            e.wait_ge(s_mm.h, R["trm"])
                            e.tensor_copy(out=payload[:, 65 * h:65 * (h + 1)],
                                          in_=P[3][:, 0:65]).then_inc(s_vc.h, 1)
                            pay_last[0] = s_vc.inc()
                            marks.set(("bank", 3), s_vc)

                    marks.set("projw", s_mm)
                    marks.set("qsb", s_mm)
                    marks.set("ksb", s_mm)
                    marks.set("vsb", s_mm)

                if DEBUG and ASUB >= 3:
                    @blk.sync
                    def _(e):
                        e.wait_ge(s_vc.h, pay_last[0])
                        e.dma_start(out=dbg["pay"][:, :], in_=payload[:]).then_inc(s_st.h, 16)
                        s_st.inc(16)

                if pend_norm[0] is not None:
                    emit_norm_finish(*pend_norm[0])
                    pend_norm[0] = None
                mid_done = s_vc.v

                if ASUB < 4:
                    final_wait[0] = (pay_last[0] if ASUB == 3 else
                                     track_mid[0] if ASUB == 2 else v_evac_last[0])
                    raise StopEmission
                @blk.sync
                def _(e):
                    e.wait_ge(s_vc.h, pay_last[0])
                    e.dma_start(out=cc_in.ap()[:, :], in_=payload[:]).then_inc(s_st.h, 16)
                    s_st.inc(16)
                pay_in = s_st.v

                @blk.gpsimd
                def _(e):
                    e.wait_ge(s_st.h, pay_in)
                    e.collective_compute("AllGather", ALU.bypass,
                                         replica_groups=[list(range(NCORES))],
                                         ins=[cc_in.ap().opt()],
                                         outs=[cc_out.ap().opt()]).then_inc(s_cc.h, 1)
                    s_cc.inc()
                cc_done = s_cc.v

                attn_scope.close()
                open_scopes.remove(attn_scope)

                ln2_scope = ExitStack()
                open_scopes.append(ln2_scope)
                inv2, mean2, ln2_done, ln2_pair, _ = ln_setup(
                    mk_sb(ln2_scope), 64, TOK, "ln2",
                    slices=[(64, 64), (128, 448), (576, 512),
                            (1088, 512), (1600, 448), (2048, 64)])
                rt_scope = ExitStack()
                open_scopes.append(rt_scope)
                sbR = mk_sb(rt_scope)
                G1N = sbR("G1N", [8, 1], F32)
                LG = sbR("LG", [8, TOK], F32)
                SCT = sbR("SCT", [128, 16 * 8], F32)
                MX = sbR("MX", [128, 16], F32)
                EXT = sbR("EXT", [128, 16 * 8], F32)
                M1T = sbR("M1T", [128, 16], F32)
                M2T = sbR("M2T", [128, 16], F32)
                MSK = sbR("MSK", [128, 16 * 8], F32)
                DEN = sbR("DEN", [128, 16], F32)
                WTM = sbR("WTM", [128, 16 * 8], F32)
                lt = {}
                exd = {}

                def emit_route_logits(tslist, rbanks, wait_xv):
                    lg_mm = {}

                    @blk.tensor
                    def _(e):
                        e.wait_ge(s_vc.h, wait_xv)
                        for i, ts_ in enumerate(tslist):
                            bank = rbanks[i]
                            marks.need(e, ("bank", bank))
                            last = None
                            for kc in range(KC):
                                last = e.matmul(P[bank][0:8, 0:512], gw_sb[:, 8 * kc:8 * (kc + 1)],
                                         XKV[:, NKV * kc + 64 + 512 * ts_:NKV * kc + 64 + 512 * (ts_ + 1)],
                                         start=(kc == 0), stop=(kc == KC - 1))
                            last.then_inc(s_mm.h, 1)
                            lg_mm[ts_] = (s_mm.inc(), bank)
                    return lg_mm

                def emit_route_fold(tslist, lg_mm, ln_waits):
                    @blk.vector
                    def _(e):
                        e.wait_ge(s_vc.h, R["g1done"])
                        for v in ln_waits:
                            e.wait_ge(s_vc.h, v)
                        for ts_ in tslist:
                            e.wait_ge(s_mm.h, lg_mm[ts_][0])
                        for ts_ in tslist:
                            _, bank = lg_mm[ts_]
                            e.scalar_tensor_tensor(out=LG[:, 512 * ts_:512 * (ts_ + 1)],
                                                   in0=mean2[0:8, 512 * ts_:512 * (ts_ + 1)],
                                                   scalar=G1N[:, 0:1],
                                                   in1=P[bank][0:8, 0:512],
                                                   op0=ALU.mult, op1=ALU.add)
                        last = None
                        for ts_ in tslist:
                            last = e.tensor_tensor(out=LG[:, 512 * ts_:512 * (ts_ + 1)],
                                            in0=LG[:, 512 * ts_:512 * (ts_ + 1)],
                                            in1=inv2[0:8, 512 * ts_:512 * (ts_ + 1)], op=ALU.mult)
                        last.then_inc(s_vc.h, 1)
                        s_vc.inc()
                        for ts_ in tslist:
                            marks.set(("bank", lg_mm[ts_][1]), s_vc)

                def emit_route_t(tlist, rbanks2):
                    v0 = s_vc.v

                    @blk.tensor
                    def _(e):
                        for idx, t in enumerate(tlist):
                            bank = rbanks2[idx % 2]
                            if idx < 2:
                                marks.need(e, ("bank", bank))
                            else:
                                e.wait_ge(s_vc.h, v0 + idx - 1)
                            e.transpose(P[bank][:, 0:8], LG[0:8, 128 * t:128 * (t + 1)],
                                        ident_sb[0:8, 0:8]).then_inc(s_mm.h, 1)
                            lt[t] = (s_mm.inc(), bank)

                    @blk.vector
                    def _(e):
                        for t in tlist:
                            v, bank = lt[t]
                            e.wait_ge(s_mm.h, v)
                            e.tensor_copy(out=SCT[:, 8 * t:8 * (t + 1)],
                                          in_=P[bank][:, 0:8]).then_inc(s_vc.h, 1)
                            s_vc.inc()
                            marks.set(("bank", bank), s_vc)
                        for t in tlist:
                            e.reduce_max(out=MX[:, t:t + 1], in_=SCT[:, 8 * t:8 * (t + 1)],
                                         axis=AX.X, negate=True).then_inc(s_vc.h, 1)
                            s_vc.inc()
                    grp_done = s_vc.v

                    @blk.scalar
                    def _(e):
                        e.wait_ge(s_vc.h, grp_done)
                        for t in tlist:
                            e.activation(out=EXT[:, 8 * t:8 * (t + 1)], in_=SCT[:, 8 * t:8 * (t + 1)],
                                         func=AF.Exp, bias=MX[:, t:t + 1], scale=1.0).then_inc(s_sc.h, 1)
                            exd[t] = s_sc.inc()

                    @blk.vector
                    def _(e):
                        e.wait_ge(s_sc.h, max(exd[t] for t in tlist))
                        for t in tlist:
                            e.reduce_max(out=M1T[:, t:t + 1], in_=EXT[:, 8 * t:8 * (t + 1)], axis=AX.X)
                        for t in tlist:
                            e.tensor_scalar(out=MSK[:, 8 * t:8 * (t + 1)], in0=EXT[:, 8 * t:8 * (t + 1)],
                                            scalar1=M1T[:, t:t + 1], scalar2=None, op0=ALU.is_ge)
                        for t in tlist:
                            e.scalar_tensor_tensor(out=WTM[:, 8 * t:8 * (t + 1)],
                                                   in0=MSK[:, 8 * t:8 * (t + 1)], scalar=-1e30,
                                                   in1=EXT[:, 8 * t:8 * (t + 1)], op0=ALU.mult, op1=ALU.add)
                        for t in tlist:
                            e.reduce_max(out=M2T[:, t:t + 1], in_=WTM[:, 8 * t:8 * (t + 1)], axis=AX.X)
                        for t in tlist:
                            e.tensor_tensor(out=DEN[:, t:t + 1], in0=M1T[:, t:t + 1],
                                            in1=M2T[:, t:t + 1], op=ALU.add)
                        for t in tlist:
                            e.tensor_scalar(out=MSK[:, 8 * t:8 * (t + 1)], in0=EXT[:, 8 * t:8 * (t + 1)],
                                            scalar1=M2T[:, t:t + 1], scalar2=None, op0=ALU.is_ge)
                        for t in tlist:
                            e.reciprocal(out=DEN[:, t:t + 1], in_=DEN[:, t:t + 1])
                        for t in tlist:
                            e.scalar_tensor_tensor(out=WTM[:, 8 * t:8 * (t + 1)],
                                                   in0=EXT[:, 8 * t:8 * (t + 1)], scalar=DEN[:, t:t + 1],
                                                   in1=MSK[:, 8 * t:8 * (t + 1)],
                                                   op0=ALU.mult, op1=ALU.mult).then_inc(s_vc.h, 1)
                            s_vc.inc()
                    wtm_grp = s_vc.v

                    wt = {}
                    v0w = s_vc.v

                    @blk.tensor
                    def _(e):
                        e.wait_ge(s_vc.h, wtm_grp)
                        for idx, t in enumerate(tlist):
                            bank = rbanks2[idx % 2]
                            if idx < 2:
                                marks.need(e, ("bank", bank))
                            else:
                                e.wait_ge(s_vc.h, v0w + idx - 1)
                            e.transpose(P[bank][0:8, 0:128], WTM[:, 8 * t:8 * (t + 1)],
                                        ident_sb[:]).then_inc(s_mm.h, 1)
                            wt[t] = (s_mm.inc(), bank)

                    @blk.vector
                    def _(e):
                        for t in tlist:
                            v, bank = wt[t]
                            e.wait_ge(s_mm.h, v)
                            e.tensor_copy(out=w_bf[:, 128 * t:128 * (t + 1)],
                                          in_=P[bank][0:8, 0:128]).then_inc(s_vc.h, 1)
                            s_vc.inc()
                            marks.set(("bank", bank), s_vc)
                    return s_vc.v

                if STAGE >= 4:
                    @blk.tensor
                    def _(e):
                        marks.need(e, ("bank", 6))
                        e.wait_ge(s_vc.h, ones_ready)
                        last = None
                        for kc in range(KC):
                            last = e.matmul(P[6][0:8, 0:1], gw_sb[:, 8 * kc:8 * (kc + 1)], onescol_f[:],
                                     start=(kc == 0), stop=(kc == KC - 1))
                        last.then_inc(s_mm.h, 1)
                        R["g1mm"] = s_mm.inc()

                    @blk.vector
                    def _(e):
                        e.wait_ge(s_mm.h, R["g1mm"])
                        e.tensor_scalar(out=G1N[:], in0=P[6][0:8, 0:1], scalar1=-1.0, scalar2=None,
                                        op0=ALU.mult).then_inc(s_vc.h, 1)
                        R["g1done"] = s_vc.inc()
                        marks.set(("bank", 6), s_vc)

                if STAGE >= 3:
                    ln2_pair([2, 3], mid_done)
                    ln2_pair([1, 4], mid_done)

                if STAGE >= 4:
                    lg_mid = emit_route_logits([1, 2], [6, 7], mid_done)
                    emit_route_fold([1, 2], lg_mid, [ln2_done[2], ln2_done[3]])
                    emit_route_t([4, 5, 6, 7, 8, 9, 10, 11], [6, 7])

                gc_scope = ExitStack()
                open_scopes.append(gc_scope)
                sbG = mk_sb(gc_scope)
                GSL = sbG("GSL", [128, 8 * PAYW], BF16)
                GACC = sbG("GACC", [128, PAYW], F32)
                GRC = sbG("GRC", [128, 16], F32)
                GN = sbG("GN", [128, H * 64], F32)

                @blk.gpsimd
                def _(e):
                    e.wait_ge(s_cc.h, cc_done)
                    for i in range(NCORES):
                        e.dma_start(out=GSL[:, PAYW * i:PAYW * (i + 1)],
                                    in_=cc_out.ap()[128 * i:128 * (i + 1), :]).then_inc(s_w.h, 16)
                        s_w.inc(16)
                gsl_in = s_w.v

                @blk.vector
                def _(e):
                    HW = 8 * 65
                    e.wait_ge(s_w.h, gsl_in)
                    e.tensor_tensor(out=GACC[:, 0:HW], in0=GSL[:, 0:HW],
                                    in1=GSL[:, PAYW:PAYW + HW], op=ALU.add)
                    e.tensor_tensor(out=GACC[:, HW:PAYW], in0=GSL[:, HW:PAYW],
                                    in1=GSL[:, PAYW + HW:2 * PAYW], op=ALU.add)
                    for i in range(2, NCORES):
                        e.tensor_tensor(out=GACC[:, 0:HW], in0=GACC[:, 0:HW],
                                        in1=GSL[:, i * PAYW:i * PAYW + HW], op=ALU.add)
                        e.tensor_tensor(out=GACC[:, HW:PAYW], in0=GACC[:, HW:PAYW],
                                        in1=GSL[:, i * PAYW + HW:(i + 1) * PAYW], op=ALU.add)
                    ga = GACC[:, 0:PAYW].rearrange("p (h c) -> p h c", c=65)
                    for hh in range(H):
                        e.reciprocal(out=GRC[:, hh:hh + 1], in_=ga[:, hh, 64:65])
                    for hh in range(H):
                        e.tensor_scalar(out=GN[:, 64 * hh:64 * (hh + 1)], in0=ga[:, hh, 0:64],
                                        scalar1=GRC[:, hh:hh + 1], scalar2=None,
                                        op0=ALU.mult).then_inc(s_vc.h, 1)
                        s_vc.inc()
                gn_done = s_vc.v

                if DEBUG:
                    @blk.sync
                    def _(e):
                        e.wait_ge(s_vc.h, gn_done)
                        e.dma_start(out=dbg["gcomb"][:, :], in_=GACC[:, 0:PAYW]).then_inc(s_st.h, 16)
                        s_st.inc(16)

                gtr = {}

                v0gt = s_vc.v

                @blk.tensor
                def _(e):
                    e.wait_ge(s_vc.h, gn_done)
                    for hh in range(H):
                        bank = 4 + hh % 2
                        if hh < 2:
                            marks.need(e, ("bank", bank))
                        else:
                            e.wait_ge(s_vc.h, v0gt + hh - 1)
                        e.transpose(P[bank][0:64, 0:128], GN[:, 64 * hh:64 * (hh + 1)],
                                    ident_sb[:]).then_inc(s_mm.h, 1)
                        gtr[hh] = (s_mm.inc(), bank)

                @blk.vector
                def _(e):
                    for hh in range(H):
                        v, bank = gtr[hh]
                        e.wait_ge(s_mm.h, v)
                        kcc = hh // 2
                        hb = 64 * (hh % 2)
                        for pos, fcol in ((0, 64), (1, 64 + 1984)):
                            xsl = XKV[hb:hb + 64,
                                      NKV * kcc + fcol:NKV * kcc + fcol + 64]
                            last = e.scalar_tensor_tensor(out=xsl,
                                                   in0=P[bank][0:64, 64 * pos:64 * pos + 64],
                                                   scalar=flags[hb:hb + 64, pos:pos + 1],
                                                   in1=xsl, op0=ALU.mult, op1=ALU.add)
                        last.then_inc(s_vc.h, 1)
                        s_vc.inc()
                        marks.set(("bank", bank), s_vc)
                xa_done = s_vc.v
                final_wait[0] = xa_done

                gc_scope.close()
                open_scopes.remove(gc_scope)

                if DEBUG:
                    @blk.sync
                    def _(e):
                        e.wait_ge(s_vc.h, xa_done)
                        for k in range(KC):
                            e.dma_start(out=dbg["xa"][128 * k:128 * (k + 1), :],
                                        in_=XKV[:, NKV * k + 64:NKV * k + 64 + TOK]).then_inc(s_st.h, 16)
                            s_st.inc(16)

                if STAGE < 3:
                    raise StopEmission
                ln2_pair([0, 5], xa_done)
                ln2_all = max(ln2_done.values())

                if DEBUG:
                    @blk.gpsimd
                    def _(e):
                        e.wait_ge(s_vc.h, ln2_all)
                        for k in range(KC):
                            e.dma_start(out=dbg["xln2"][128 * k:128 * (k + 1), :],
                                        in_=XLN[:, NKV * k + 64:NKV * k + 64 + TOK]).then_inc(s_st.h, 16)
                            s_st.inc(16)

                final_wait[0] = ln2_all
                if STAGE < 4:
                    raise StopEmission

                lg_edge = emit_route_logits([0, 3], [4, 5], xa_done)
                emit_route_fold([0, 3], lg_edge,
                                [ln2_done[0], ln2_done[1], ln2_done[4], ln2_done[5]])
                wbf_done = emit_route_t([0, 1, 2, 3, 12, 13, 14, 15], [4, 5])

                if DEBUG:
                    @blk.sync
                    def _(e):
                        e.wait_ge(s_vc.h, wbf_done)
                        e.dma_start(out=dbg["wroute"][:, :], in_=WTM[:]).then_inc(s_st.h, 16)
                        e.dma_start(out=dbg["lg"][:, :], in_=LG[:]).then_inc(s_st.h, 16)
                        e.dma_start(out=dbg["sct"][:, :], in_=SCT[:]).then_inc(s_st.h, 16)
                        s_st.inc(32)

                rt_scope.close()
                open_scopes.remove(rt_scope)
                ln2_scope.close()
                open_scopes.remove(ln2_scope)
                final_wait[0] = wbf_done

                if STAGE < 5:
                    raise StopEmission
                moe_scope = ExitStack()
                open_scopes.append(moe_scope)
                sbM = mk_sb(moe_scope)
                WGB = sbM("WGB", [128, 2 * 8 * 384], BF16)
                WUB = sbM("WUB", [128, 2 * 8 * 384], BF16)
                WDB = sbM("WDB", [128, IC_E * 1024], BF16)
                HND = sbM("HND", [128, 19 * 512], BF16)
                EHND = sbM("EHND", [128, 2 * 6 * 512], BF16)
                WBC = sbM("WBC", [128, 2 * 512], BF16)
                SGB = sbM("SGB", [128, 2 * 512], BF16)
                wgb3 = WGB[:].rearrange("p (b k c) -> p b k c", b=2, k=8)
                wub3 = WUB[:].rearrange("p (b k c) -> p b k c", b=2, k=8)
                wdb3 = WDB[:].rearrange("p (i c) -> p i c", c=1024)

                def icgroups(nic, first=3):
                    out = []
                    c = 0
                    while c < nic:
                        n = min(first if c == 0 else 3, nic - c)
                        out.append((c, n))
                        c += n
                    return out

                passes = []
                for ex in (-1, -2):
                    passes.append((ex, [0, 1]))
                    passes.append((ex, [2, 3]))
                for ex in range(8):
                    passes.append((ex, [0, 1]))
                    passes.append((ex, [2, 3]))
                if MOE_ONLY == 'shared':
                    passes = passes[:4]
                elif isinstance(MOE_ONLY, int):
                    passes = [p_ for p_ in passes if p_[0] == MOE_ONLY]

                gseq_box = [0]
                pending = [None]

                def hnd_ap(par, ntiles, ic, ti, g0):
                    if ic < g0:
                        sl = par * 6 + ic * ntiles + ti
                        return EHND[:, 512 * sl:512 * (sl + 1)]
                    sl = (ic - g0) * ntiles + ti
                    return HND[:, 512 * sl:512 * (sl + 1)]

                def emit_group(pid, ex, tiles, par, g_i, c0, n, wbc_done, hm_box, g0):
                    ntiles = len(tiles)
                    slot = gseq_box[0] % 2
                    gseq_cur = gseq_box[0]
                    gseq_box[0] += 1

                    @blk.gpsimd
                    def _(e):
                        if gseq_cur < 2:
                            e.wait_ge(s_vc.h, wbf_done)
                        marks.need(e, ("wgu", slot))
                        if ex < 0:
                            so = c0 + (0 if ex == -1 else IC_E)
                            sg = swg_d[:, 128 * so:128 * (so + n)]
                            su = swu_d[:, 128 * so:128 * (so + n)]
                        else:
                            sg = wg_d[ex, :, 128 * c0:128 * (c0 + n)]
                            su = wu_d[ex, :, 128 * c0:128 * (c0 + n)]
                        e.dma_start(out=wgb3[:, slot, :, 0:128 * n],
                                    in_=sg.rearrange("(k p) c -> p k c", p=128)).then_inc(s_w.h, 16)
                        s_w.inc(16)
                        e.dma_start(out=wub3[:, slot, :, 0:128 * n],
                                    in_=su.rearrange("(k p) c -> p k c", p=128)).then_inc(s_w.h, 16)
                        s_w.inc(16)
                    gu_w = s_w.v

                    gu_mm = {}
                    ssc0 = s_sc.v
                    vb0 = s_vc.v

                    def uwv(j):
                        return vb0 + (1 if j == 0 else 2 * j)

                    @blk.tensor
                    def _(e):
                        e.wait_ge(s_w.h, gu_w)
                        if pid == 0 and g_i == 0:
                            e.wait_ge(s_vc.h, ln2_all)
                        for ti, mt in enumerate(tiles):
                            tcol = 64 + 512 * mt
                            for ic in range(n):
                                j = ti * n + ic
                                gbank = j % 2
                                ubank = 2 + j % 2
                                if j < 2:
                                    marks.need(e, ("bank", gbank))
                                    marks.need(e, ("bank", ubank))
                                else:
                                    e.wait_ge(s_sc.h, ssc0 + j - 1)
                                    e.wait_ge(s_vc.h, uwv(j - 2))
                                last = None
                                for kc in range(KC):
                                    last = e.matmul(P[gbank][:, 0:512],
                                             wgb3[:, slot, kc, 128 * ic:128 * (ic + 1)],
                                             XLN[:, NKV * kc + tcol:NKV * kc + tcol + 512],
                                             start=(kc == 0), stop=(kc == KC - 1))
                                last.then_inc(s_mm.h, 1)
                                gv = s_mm.inc()
                                last = None
                                for kc in range(KC):
                                    last = e.matmul(P[ubank][:, 0:512],
                                             wub3[:, slot, kc, 128 * ic:128 * (ic + 1)],
                                             XLN[:, NKV * kc + tcol:NKV * kc + tcol + 512],
                                             start=(kc == 0), stop=(kc == KC - 1))
                                last.then_inc(s_mm.h, 1)
                                gu_mm[j] = (gv, s_mm.inc())
                        marks.set(("wgu", slot), s_mm)

                    nj = len(tiles) * n
                    sgd = {}

                    @blk.scalar
                    def _(e):
                        for j in range(nj):
                            gv, uv = gu_mm[j]
                            e.wait_ge(s_mm.h, gv)
                            if j < 2:
                                marks.need(e, ("sgb", j % 2))
                            else:
                                e.wait_ge(s_vc.h, vb0 + 2 * j - 1)
                            e.activation(out=SGB[:, 512 * (j % 2):512 * (j % 2 + 1)],
                                         in_=P[j % 2][:, 0:512],
                                         func=(AF.Identity if MOE_DIAG == 1 else AF.Silu)).then_inc(s_sc.h, 1)
                            sgd[j] = s_sc.inc()
                            marks.set(("bank", j % 2), s_sc)

                    @blk.vector
                    def _(e):
                        if ex >= 0 and g_i == 0 and wbc_done is not None:
                            e.wait_ge(s_vc.h, wbc_done)
                        if g_i == 0:
                            marks.need(e, ("ehnd", par))
                        if g_i == 1:
                            marks.need(e, ("hnd", 0))

                        def emit_h(j):
                            e.wait_ge(s_sc.h, sgd[j])
                            ti, ic = divmod(j, n)
                            ap = hnd_ap(par, len(tiles), c0 + ic, ti, g0)
                            if MOE_DIAG == 2:
                                e.tensor_scalar(out=ap, in0=ap, scalar1=1.0, scalar2=None,
                                                op0=ALU.mult).then_inc(s_vc.h, 1)
                            else:
                                e.tensor_tensor(out=ap,
                                                in0=SGB[:, 512 * (j % 2):512 * (j % 2 + 1)],
                                                in1=ap, op=ALU.mult).then_inc(s_vc.h, 1)
                            s_vc.inc()
                            marks.set(("sgb", j % 2), s_vc)

                        prev = None
                        for j in range(nj):
                            ti, ic = divmod(j, n)
                            gv, uv = gu_mm[j]
                            e.wait_ge(s_mm.h, uv)
                            ap = hnd_ap(par, len(tiles), c0 + ic, ti, g0)
                            if ex >= 0:
                                last = e.tensor_tensor(out=ap, in0=P[2 + j % 2][:, 0:512],
                                                       in1=WBC[:, 512 * ti:512 * (ti + 1)],
                                                       op=ALU.mult)
                            else:
                                last = e.tensor_copy(out=ap, in_=P[2 + j % 2][:, 0:512])
                            last.then_inc(s_vc.h, 1)
                            s_vc.inc()
                            marks.set(("bank", 2 + j % 2), s_vc)
                            if prev is not None:
                                emit_h(prev)
                            prev = j
                        emit_h(prev)
                    hm_box[0] = s_vc.v

                def emit_wd(ex, tiles, par, hm_last, g0):
                    for h0, hn in [(0, IC_E)]:
                        wd_ready2 = s_w.v

                        for ti, mt in enumerate(tiles):
                            tcol = 64 + 512 * mt
                            for oc in range(8):
                                @blk.tensor
                                def _(e):
                                    bank = 6 + oc % 2
                                    marks.need(e, ("bank", bank))
                                    e.wait_ge(s_w.h, wd_ready2)
                                    e.wait_ge(s_vc.h, hm_last)
                                    last = None
                                    for icl in range(hn):
                                        ic = h0 + icl
                                        last = e.matmul(P[bank][:, 0:512],
                                                 wdb3[:, icl, 128 * oc:128 * (oc + 1)],
                                                 hnd_ap(par, len(tiles), ic, ti, g0),
                                                 start=(icl == 0), stop=(icl == hn - 1))
                                    last.then_inc(s_mm.h, 1)
                                    R["ymm"] = s_mm.inc()

                                @blk.vector
                                def _(e):
                                    bank = 6 + oc % 2
                                    e.wait_ge(s_mm.h, R["ymm"])
                                    xsl = XKV[:, NKV * oc + tcol:NKV * oc + tcol + 512]
                                    e.tensor_tensor(out=xsl, in0=P[bank][:, 0:512], in1=xsl,
                                                    op=ALU.add).then_inc(s_vc.h, 1)
                                    s_vc.inc()
                                    marks.set(("bank", bank), s_vc)
                        marks.set("wdb", s_mm)
                    marks.set(("ehnd", par), s_mm)
                    marks.set(("hnd", 0), s_mm)

                for pid, (ex, tiles) in enumerate(passes):
                    nic = IC_E
                    gl = icgroups(nic, 3)
                    par = pid % 2
                    g0sz = gl[0][1]

                    wbc_done = None
                    if ex >= 0:
                        wbmm = {}

                        @blk.tensor
                        def _(e):
                            if pid == 4 or (pid == 0 and isinstance(MOE_ONLY, int)):
                                e.wait_ge(s_vc.h, wbf_done)
                                e.wait_ge(s_w.h, sel_w)
                            for ti, mt in enumerate(tiles):
                                bank = 4 + ti
                                marks.need(e, ("bank", bank))
                                e.matmul(P[bank][:, 0:512], sel_sb[:, 128 * ex:128 * (ex + 1)],
                                         w_bf[:, 512 * mt:512 * (mt + 1)],
                                         start=True, stop=True).then_inc(s_mm.h, 1)
                                wbmm[ti] = (s_mm.inc(), bank)

                        @blk.vector
                        def _(e):
                            marks.need(e, ("wbc", 0))
                            for ti in range(len(tiles)):
                                v, bank = wbmm[ti]
                                e.wait_ge(s_mm.h, v)
                                e.tensor_copy(out=WBC[:, 512 * ti:512 * (ti + 1)],
                                              in_=P[bank][:, 0:512]).then_inc(s_vc.h, 1)
                                s_vc.inc()
                                marks.set(("bank", bank), s_vc)
                        wbc_done = s_vc.v

                    hm_box = [None]
                    emit_group(pid, ex, tiles, par, 0, gl[0][0], gl[0][1], wbc_done, hm_box, g0sz)

                    if pending[0] is not None and MOE_DIAG != 3:
                        emit_wd(*pending[0])

                    for g_i in range(1, len(gl)):
                        c0, n = gl[g_i]
                        emit_group(pid, ex, tiles, par, g_i, c0, n, wbc_done, hm_box, g0sz)

                    @blk.gpsimd
                    def _(e):
                        if pid == 0:
                            e.wait_ge(s_vc.h, wbf_done)
                        marks.need(e, "wdb")
                        src = swd_d if ex < 0 else wd_d[ex]
                        wo = 0 if ex >= -1 else IC_E
                        for (c0, n) in icgroups(IC_E):
                            e.dma_start(out=wdb3[:, c0:c0 + n, :],
                                        in_=src[128 * (wo + c0):128 * (wo + c0 + n), :]
                                        .rearrange("(i p) c -> p i c", p=128)).then_inc(s_w.h, 16)
                            s_w.inc(16)

                    if ex >= 0 and len(gl):
                        marks.set(("wbc", 0), s_vc, hm_box[0])
                    pending[0] = (ex, tiles, par, hm_box[0], g0sz)

                if pending[0] is not None and MOE_DIAG != 3:
                    emit_wd(*pending[0])
                moe_all = s_vc.v
                moe_scope.close()
                open_scopes.remove(moe_scope)
                final_wait[0] = moe_all

            open_scopes = []
            try:
                emit_rest()
            except StopEmission:
                for sc_ in reversed(open_scopes):
                    sc_.close()

            @blk.sync
            def _(e):
                e.wait_ge(s_vc.h, final_wait[0])
                for k in range(KC):
                    e.dma_start(out=yout[128 * k:128 * (k + 1), :],
                                in_=XKV[:, NKV * k + 64:NKV * k + 64 + TOK]).then_inc(s_st.h, 16)
                    s_st.inc(16)
                e.wait_ge(s_st.h, s_st.v)


    return nc, dbg


def _host_inputs(inputs):
    x = np.asarray(inputs["L0_attn_hidden_states"])[0]
    S = x.shape[0]
    bq = np.asarray(inputs["L0_attn_bq"])
    bk = np.asarray(inputs["L0_attn_bk"])
    bv = np.asarray(inputs["L0_attn_bv"])
    gate = np.asarray(inputs["L0_ffn_gate_w"])

    wqT = np.ascontiguousarray(np.asarray(inputs["L0_attn_Wq"]).T)
    wkT = np.ascontiguousarray(np.asarray(inputs["L0_attn_Wk"]).T)
    wvT = np.ascontiguousarray(np.asarray(inputs["L0_attn_Wv"]).T)
    b3 = np.concatenate([bq.reshape(8, 128).T, bk.reshape(8, 128).T,
                         bv.reshape(8, 128).T], axis=1).astype(np.float32)
    bvr = np.ascontiguousarray(bv.reshape(1, 1024))
    gw8 = np.ascontiguousarray(gate.T.reshape(8, 128, 8).transpose(1, 0, 2).reshape(128, 64))
    identf = np.eye(128, dtype=np.float32)

    kcorn = np.zeros((2, 128), np.float32)
    kcorn[0, 64:128] = 1.0
    kcorn[1, 0:64] = 1.0
    qcorn = np.zeros((2, 5 * GQ), np.float32)
    for c in range(5):
        jA = 2 * c - 2
        if 0 <= jA < 8:
            qcorn[0, GQ * c + 64 * jA:GQ * c + 64 * (jA + 1)] = -30000.0
        jB = 2 * c + 1
        if 0 <= jB < 8:
            qcorn[1, GQ * c + 64 * jB:GQ * c + 64 * (jB + 1)] = -30000.0
    kfix = np.zeros((2, 128), np.float32)
    kfix[0, 0:64] = 1.0
    kfix[1, 64:128] = 1.0
    hfix = np.zeros((1, NKV), np.float32)
    hfix[0, 0:64] = 1.0
    hfix[0, 2112:2176] = 1.0
    sel = np.zeros((8, 8 * 128), np.float32)
    for e_ in range(8):
        sel[e_, 128 * e_:128 * (e_ + 1)] = 1.0

    shared = dict(wqT=wqT, wkT=wkT, wvT=wvT, b3=b3, bvr=bvr, gw8=gw8,
                  wg=np.ascontiguousarray(np.asarray(inputs["L0_ffn_wg"])),
                  wu=np.ascontiguousarray(np.asarray(inputs["L0_ffn_wu"])),
                  wd=np.ascontiguousarray(np.asarray(inputs["L0_ffn_wd"])),
                  swg=np.ascontiguousarray(np.asarray(inputs["L0_ffn_swg"])),
                  swu=np.ascontiguousarray(np.asarray(inputs["L0_ffn_swu"])),
                  swd=np.ascontiguousarray(np.asarray(inputs["L0_ffn_swd"])),
                  identf=identf, kcorn=kcorn, qcorn=qcorn, kfix=kfix,
                  hfix=hfix, sel=sel)

    in_maps = []
    for ci in range(NCORES):
        t0 = ci * TOK
        prev = x[t0 - 64:t0] if ci > 0 else x[0:64]
        nxt = x[t0 + TOK:t0 + TOK + 64] if ci < 7 else x[0:64]
        kvx = np.concatenate([prev, x[t0:t0 + TOK], nxt, x[0:64], x[S - 64:S]], 0)
        xin = np.ascontiguousarray(kvx.T)

        qfix = np.zeros((2, TOK), np.float32)
        if ci == 0:
            qfix[0, 64:128] = -30000.0
        if ci == 7:
            qfix[1, 1920:1984] = -30000.0
        zf = np.ones((1, TOK), np.float32)
        if ci == 0:
            zf[0, 0:64] = 0.0
        if ci == 7:
            zf[0, TOK - 64:] = 0.0
        fl = np.zeros((128, 2), np.float32)
        fl[:, 0] = 1.0 if ci == 0 else 0.0
        fl[:, 1] = 1.0 if ci == 7 else 0.0
        m = dict(shared)
        m.update(xin=xin, qfix=qfix, zflagi=zf, flags=fl,
                 qnegi=np.full((1, 128), -30000.0, np.float32))
        in_maps.append(m)
    return in_maps


_CACHE = {}


def kernel(**inputs):
    if "nc" not in _CACHE:
        _CACHE["nc"], _CACHE["dbg"] = build_program()
    nc = _CACHE["nc"]
    in_maps = _host_inputs(inputs)
    res = run_bass_kernel_spmd(nc, in_maps, core_ids=list(range(NCORES)))
    kernel_last_results[0] = res
    outs = [np.ascontiguousarray(res.results[ci]["y"].T) for ci in range(NCORES)]
    return np.concatenate(outs, 0)[None].astype(np.float32)


kernel_last_results = [None]
